# version 16
# baseline (speedup 1.0000x reference)
import numpy as np
from contextlib import ExitStack

import concourse.bass as bass
import concourse.mybir as mybir
from concourse.bass_utils import run_bass_kernel_spmd

F32 = mybir.dt.float32
BF16 = mybir.dt.bfloat16
AF = mybir.ActivationFunctionType
ALU = mybir.AluOpType

SEQ = 4096
DM = 1024
NG = 8
GT = 512
NCH = 8
NKT = 32
VW = 65

ENGS = ("tensor", "scalar", "vector", "gpsimd", "sync")


class Sched:

    def __init__(self):
        self.items = {e: [] for e in ENGS}
        self.cnt = {}
        self.pending = {e: [] for e in ENGS}

    def op(self, eng, emit, waits=(), sem=None, amt=1):
        if sem is None:
            sem = eng
        self.cnt.setdefault(sem, 0)
        w = [x for x in self.pending[eng] if x is not None]
        self.pending[eng] = []
        w += [x for x in waits if x is not None]
        self.items[eng].append((w, emit, sem, amt))
        self.cnt[sem] += amt
        return (sem, self.cnt[sem])

    def wait_only(self, eng, waits):
        self.items[eng].append(([x for x in waits if x is not None], None, None, 0))

    def barrier(self, engs=ENGS):
        marks = [(s, c) for s, c in self.cnt.items() if c > 0]
        for e in engs:
            self.pending[e] = marks[:]


def build_nc():
    nc = bass.Bass("TRN2", target_bir_lowering=False)

    XT = nc.dram_tensor("xt", [DM, SEQ], F32, kind="ExternalInput").ap()
    WQ = nc.dram_tensor("wq", [DM, 3 * DM], F32, kind="ExternalInput").ap()
    QB = nc.dram_tensor("qb", [128, 24], F32, kind="ExternalInput").ap()
    VBR = nc.dram_tensor("vbr", [1, DM], F32, kind="ExternalInput").ap()
    OW = nc.dram_tensor("ow", [DM, DM], F32, kind="ExternalInput").ap()
    OB = nc.dram_tensor("ob", [128, 8], F32, kind="ExternalInput").ap()
    W1 = nc.dram_tensor("w1", [DM, 2048], F32, kind="ExternalInput").ap()
    B1 = nc.dram_tensor("b1", [128, 16], F32, kind="ExternalInput").ap()
    W2 = nc.dram_tensor("w2", [2048, DM], F32, kind="ExternalInput").ap()
    B2 = nc.dram_tensor("b2", [128, 8], F32, kind="ExternalInput").ap()
    TRI = nc.dram_tensor("tri", [128, 128], F32, kind="ExternalInput").ap()
    OUT = nc.dram_tensor("outT", [DM, SEQ], F32, kind="ExternalOutput").ap()

    WQ_S = nc.dram_tensor("wq_s", [DM, 3 * DM], BF16, kind="Internal").ap()
    OW_S = nc.dram_tensor("ow_s", [DM, DM], BF16, kind="Internal").ap()
    W1_S = nc.dram_tensor("w1_s", [DM, 2048], BF16, kind="Internal").ap()
    W2_S = nc.dram_tensor("w2_s", [2048, DM], BF16, kind="Internal").ap()
    SCR_R = nc.dram_tensor("scr_r", [1, 1024], F32, kind="Internal").ap()

    es = ExitStack()
    sb = lambda name, shape, dt: es.enter_context(nc.sbuf_tensor(name, shape, dt))[:]
    ps = lambda name, shape: es.enter_context(nc.psum_tensor(name, shape, F32))[:]

    k_all = sb("k_all", [128, 8 * SEQ], BF16)
    v_all = sb("v_all", [128, NKT * 16 * VW], BF16)
    qc2 = sb("qc2", [128, 2 * SEQ], BF16)
    xs = sb("xs", [128, NCH * GT], F32)
    lnb = sb("lnb", [128, NCH * GT], BF16)
    wwin = sb("wwin", [128, 2 * 1024], BF16)
    owin = sb("owin", [128, 2 * 1024], BF16)
    ebuf = sb("ebuf", [128, 2 * 1024], BF16)
    rb_sb = sb("rb_sb", [64, 1024], F32)
    sq = sb("sq", [128, 2 * 512], F32)
    tmp_ln = sb("tmp_ln", [128, 512], F32)
    vbias = sb("vbias", [128, DM], F32)
    tri = sb("tri_sb", [128, 128], BF16)
    qkvb = sb("qkvb", [128, 24], F32)
    kbias = sb("kbias", [128, 8], F32)
    outb = sb("outb", [128, 8], F32)
    b1t = sb("b1t", [128, 16], F32)
    b2t = sb("b2t", [128, 8], F32)
    stats = sb("stats", [1, 3 * 512], F32)
    rbuf = sb("rbuf", [1, 1024], F32)
    ones_c = sb("ones_c", [128, 1], F32)
    ones_r = sb("ones_r", [1, 128], F32)
    eps_t = sb("eps_t", [1, 1], F32)

    sc0 = ps("sc0", [128, 1024])
    sc1 = ps("sc1", [128, 1024])
    pva0 = ps("pva0", [128, 512])
    pvb0 = ps("pvb0", [128, 512])
    pva1 = ps("pva1", [128, 512])
    pvb1 = ps("pvb1", [128, 512])
    banks = [sc0[:, 0:512], sc0[:, 512:1024], sc1[:, 0:512], sc1[:, 512:1024],
             pva0, pvb0, pva1, pvb1]

    sems = {}
    for s in ("tensor", "scalar", "vector", "gpsimd", "dx", "dw0", "dw1", "dg",
              "dgq0", "dgq1", "dgq2", "dout", "dbc"):
        sems[s] = es.enter_context(nc.semaphore(s + "_sem"))

    S = Sched()
    st = {
        "bank_free": [None] * 8,
        "q_ready": [None] * 8,
        "ctx_ready": [None] * 8,
        "pv_free": [None, None],
        "exp_buf": [None, None],
        "pat": 0,
        "x_free": None,
        "x2_ready": None,
        "pend_pv": None,
        "pend_epi": None,
        "rbuf_free": None,
        "rb_free": None,
        "scr_free": None,
        "k_ready": [None] * NG,
        "v_ready": [None] * NG,
        "lnb_free": None,
        "qc2_free": None,
        "ln_free": None,
        "stats_free": None,
        "sq_war": [None, None],
        "sq_scalar_user": None,
        "wwin_war": [None, None],
        "owin_war": [None, None],
    }

    def MM(out, lhsT, rhs, start, stop):
        return lambda e, o=out, l=lhsT, r=rhs, s1=start, s2=stop: e.matmul(
            o, l, r, start=s1, stop=s2)

    def ACT(out, in_, func, bias=None, scale=None):
        kw = {}
        if bias is not None:
            kw["bias"] = bias
        if scale is not None:
            kw["scale"] = scale
        return lambda e, o=out, i=in_, f=func, k=dict(kw): e.activation(out=o, in_=i, func=f, **k)

    def TT(out, a, b, op):
        return lambda e, o=out, x=a, y=b, p=op: e.tensor_tensor(o, x, y, op=p)

    def STT(out, in0, scalar, in1, op0, op1):
        return lambda e, o=out, a=in0, s=scalar, b=in1, p0=op0, p1=op1: \
            e.scalar_tensor_tensor(o, a, s, b, op0=p0, op1=p1)

    def DMA(out, in_):
        return lambda e, o=out, i=in_: e.dma_start(out=o, in_=i)

    def CP(out, in_):
        return lambda e, o=out, i=in_: e.tensor_copy(o, i)

    def RC(out, in_):
        return lambda e, o=out, i=in_: e.reciprocal(o, i)

    def MS(ap, val):
        return lambda e, a=ap, v=val: e.memset(a, v)

    def MUL(out, in_, m):
        return lambda e, o=out, i=in_, v=m: e.mul(o, i, v)

    pair3 = lambda ap: ap.rearrange("p (o c) -> p o c", c=512)

    def dmy(out, waits, extra=()):
        S.op("tensor", MM(out, ones_r[0:1, 0:64], ones_r[0:1, 64:128], True, True),
             waits=list(waits) + list(extra))

    wq_ready3 = [S.op("gpsimd", DMA(WQ_S[:, b * 1024:(b + 1) * 1024],
                                    WQ[:, b * 1024:(b + 1) * 1024]),
                      sem="dgq%d" % b, amt=16) for b in range(3)]
    S.op("gpsimd", DMA(tri, TRI), sem="dg", amt=16)
    S.op("gpsimd", DMA(OW_S, OW), sem="dg", amt=16)
    S.op("gpsimd", DMA(W1_S, W1), sem="dg", amt=16)
    all_scratch = S.op("gpsimd", DMA(W2_S, W2), sem="dg", amt=16)
    tri_ld = ow_ready = w1_ready = w2_ready = all_scratch

    S.op("sync", DMA(qkvb, QB), sem="dx", amt=16)
    S.op("sync", DMA(rbuf[0:1, 0:1024], VBR), sem="dx", amt=16)
    S.op("sync", DMA(outb, OB), sem="dx", amt=16)
    S.op("sync", DMA(b1t, B1), sem="dx", amt=16)
    d_bias_all = S.op("sync", DMA(b2t, B2), sem="dx", amt=16)
    d_qb = d_vbr = d_bias_all

    v_oc = S.op("vector", MS(ones_c, 1.0 / DM))
    S.op("vector", MS(eps_t, 1e-5))
    v_or = S.op("vector", MS(ones_r, 1.0))
    vones_view = v_all.rearrange("p (t h c) -> p t h c", h=16, c=VW)[:, :, :, 64:65]
    v_vo = S.op("vector", MS(vones_view, 1.0))


    t_vb1 = S.op("tensor", MM(pva0, ones_r[0:1, :], rbuf[0:1, 0:512], True, True),
                 waits=[v_or, d_vbr])
    t_vb2 = S.op("tensor", MM(pvb0, ones_r[0:1, :], rbuf[0:1, 512:1024], True, True))
    S.op("vector", CP(vbias[:, 0:512], pva0), waits=[t_vb1])
    v_vb = S.op("vector", CP(vbias[:, 512:1024], pvb0), waits=[t_vb2])
    S.barrier()

    def emit_ln(src_dram, g, x_wait, chunk_ready=None):
        if src_dram is None:
            xld = list(chunk_ready)
        else:
            last = None
            for c in range(NCH):
                last = S.op("sync", DMA(xs[:, c * 512:(c + 1) * 512],
                                        src_dram[c * 128:(c + 1) * 128, g * GT:(g + 1) * GT]),
                            waits=[x_wait], sem="dx", amt=16)
                if c % 2 == 1:
                    dmy(pva1[0:64, 0:64], [last], [st["bank_free"][6]])
            xld = [last] * NCH
        t_sum = t_sq = None
        for c in range(NCH):
            xc = xs[:, c * 512:(c + 1) * 512]
            sqc = sq[:, (c % 2) * 512:((c % 2) + 1) * 512]
            v_s = S.op("vector", TT(sqc, xc, xc, ALU.mult),
                       waits=[xld[c], st["sq_war"][c % 2], st["sq_scalar_user"]])
            w = [xld[c]]
            if c == 0:
                w += [st["bank_free"][4], st["bank_free"][5]]
            t_sum = S.op("tensor", MM(pva0[0:1, :], ones_c, xc, c == 0, c == 7),
                         waits=w)
            t_sq = S.op("tensor", MM(pvb0[0:1, :], ones_c, sqc, c == 0, c == 7),
                        waits=[v_s])
            st["sq_war"][c % 2] = t_sq
        a_sq = S.op("scalar", (lambda e, o=stats[0:1, 1024:1536], i=pva0[0:1, :]:
                               e.activation(out=o, in_=i, func=AF.Square)),
                    waits=[t_sum, t_sq, st["stats_free"], st["ln_free"]])
        v_vr = S.op("vector", TT(stats[0:1, 512:1024], pvb0[0:1, :],
                                 stats[0:1, 1024:1536], ALU.subtract),
                    waits=[a_sq, st["stats_free"]])
        dmy(pva1[0:64, 0:64], [a_sq], [st["bank_free"][6]])
        a_ln = S.op("scalar", ACT(stats[0:1, 1024:1536], stats[0:1, 512:1024],
                                  AF.Ln, bias=eps_t[0:1, 0:1]), waits=[v_vr])
        a_rs = S.op("scalar", ACT(stats[0:1, 512:1024], stats[0:1, 1024:1536],
                                  AF.Exp, scale=-0.5))
        dmy(pva1[0:64, 0:64], [a_ln])
        v_mr = S.op("vector", TT(stats[0:1, 0:512], pva0[0:1, :],
                                 stats[0:1, 512:1024], ALU.mult),
                    waits=[a_rs, st["stats_free"]])
        a_m = a_q = a_rs
        t_b1 = S.op("tensor", MM(pva1, ones_r[0:1, :], stats[0:1, 512:1024], True, True),
                    waits=[a_rs, st["bank_free"][6]])
        t_b2 = S.op("tensor", MM(pvb1, ones_r[0:1, :], stats[0:1, 0:512], True, True),
                    waits=[v_mr, st["bank_free"][7]])
        norm = []
        for c in range(NCH):
            xc = xs[:, c * 512:(c + 1) * 512]
            S.op("vector", TT(tmp_ln, xc, pva1, ALU.mult),
                 waits=[t_b1, st["lnb_free"] if c == 0 else None])
            v2 = S.op("vector", TT(lnb[:, c * 512:(c + 1) * 512], tmp_ln, pvb1,
                                   ALU.subtract), waits=[t_b2])
            norm.append(v2)
        st["bank_free"][4] = v_mr
        st["bank_free"][5] = v_mr
        st["bank_free"][6] = norm[-1]
        st["bank_free"][7] = norm[-1]
        st["stats_free"] = t_b2
        st["ln_free"] = v_mr
        return norm

    def phase_A(g):
        norm = emit_ln(XT, g, st["x_free"])
        wwin_war = st["wwin_war"]
        for b in range(3):
            stops = [None] * 8
            for c in range(NCH):
                slot = c % 2
                wv = wwin[:, slot * 1024:(slot + 1) * 1024]
                d = S.op("scalar", DMA(wv, WQ_S[c * 128:(c + 1) * 128, b * 1024:(b + 1) * 1024]),
                         waits=[wq_ready3[b], wwin_war[slot]], sem="dw%d" % slot, amt=16)
                for j in range(8):
                    if b < 2:
                        lhsT = wv[:, j * 128:(j + 1) * 128]
                        rhs = lnb[:, c * 512:(c + 1) * 512]
                    else:
                        tc, half = j // 2, j % 2
                        lhsT = lnb[:, c * 512 + tc * 128: c * 512 + (tc + 1) * 128]
                        rhs = wv[:, half * 512:(half + 1) * 512]
                    w = [d, norm[c]]
                    if c == 0:
                        w.append(st["bank_free"][j])
                    stops[j] = S.op("tensor", MM(banks[j], lhsT, rhs, c == 0, c == 7),
                                    waits=w)
                wwin_war[slot] = stops[7]
            for j in range(8):
                if b == 0:
                    e = S.op("vector", (lambda ee, o=qc2[:, j * 512:(j + 1) * 512],
                                        i=banks[j], bb=qkvb[:, j:j + 1]:
                                        ee.tensor_scalar(o, i, bb, None, op0=ALU.add)),
                             waits=[stops[j], st["qc2_free"] if j == 0 else None])
                    st["q_ready"][j] = e
                elif b == 1:
                    e = S.op("vector", (lambda ee, o=k_all[:, j * SEQ + g * 512: j * SEQ + (g + 1) * 512],
                                        i=banks[j], bb=qkvb[:, 8 + j:9 + j]:
                                        ee.tensor_scalar(o, i, bb, 0.125, op0=ALU.add,
                                                         op1=ALU.mult)),
                             waits=[stops[j]])
                    st["k_ready"][g] = e
                else:
                    tc, half = j // 2, j % 2
                    base = (g * 4 + tc) * (16 * VW) + half * 8 * VW
                    dst = v_all[:, base: base + 8 * VW].rearrange(
                        "p (h c) -> p h c", c=VW)[:, :, 0:64]
                    srcv = banks[j].rearrange("p (h c) -> p h c", c=64)
                    vbv = vbias[:, half * 512:(half + 1) * 512].rearrange(
                        "p (h c) -> p h c", c=64)
                    e = S.op("vector", TT(dst, srcv, vbv, ALU.add),
                             waits=[stops[j], v_vo if g == 0 else None])
                st["bank_free"][j] = e
                if b == 2:
                    st["v_ready"][g] = e
        st["lnb_free"] = stops[7]

    tri_b = tri.rearrange("p (o c) -> p o c", c=128).broadcast_to([128, 2, 128])

    def flush_pv():
        if st["pend_pv"] is not None:
            f = st["pend_pv"]
            st["pend_pv"] = None
            f()
        if st["pend_epi"] is not None:
            f = st["pend_epi"]
            st["pend_epi"] = None
            f()

    def phase_B(g):
        nkt = 4 * g + 4
        first_sc = [True, True]
        first_pv = [True, True]
        for r in range(3):
            pin = r % 2
            pout = 1 - pin
            for j in range(8):
                sset = j % 2
                pva_s = pva0 if sset == 0 else pva1
                pvb_s = pvb0 if sset == 0 else pvb1
                pv_stop = [None]
                for kb in range(nkt):
                    i = st["pat"]
                    b = i % 2
                    scb = sc0 if b == 0 else sc1
                    jd = kb - 4 * g
                    C0 = 128 * jd if jd >= 0 else 0
                    qcol = pin * SEQ + j * 512
                    w_rhs = st["ctx_ready"][j] if r > 0 else st["q_ready"][j]
                    w_sc = [st["exp_buf"][b], w_rhs, st["k_ready"][kb // 4]]
                    if first_sc[b]:
                        first_sc[b] = False
                        w_sc += [st["bank_free"][2 * b], st["bank_free"][2 * b + 1]]
                    def MMT(out, lhsT, rhs, tp):
                        return lambda e, o=out, l=lhsT, r=rhs, t=tp: e.matmul(
                            o, l, r, start=True, stop=True, tile_position=t)
                    t_e = S.op("tensor", MMT(
                        scb[:, C0:512],
                        k_all[0:64, j * SEQ + kb * 128: j * SEQ + (kb + 1) * 128],
                        qc2[0:64, qcol + C0: qcol + 512], (0, 0)),
                        waits=w_sc)
                    t_o = S.op("tensor", MMT(
                        scb[:, 512 + C0:1024],
                        k_all[64:128, j * SEQ + kb * 128: j * SEQ + (kb + 1) * 128],
                        qc2[64:128, qcol + C0: qcol + 512], (64, 0)))
                    flush_pv()
                    ev = pair3(ebuf[:, b * 1024:(b + 1) * 1024])
                    s_e = S.op("scalar", ACT(ev[:, :, C0:512],
                                             pair3(scb)[:, :, C0:512], AF.Exp),
                               waits=[t_o])
                    st["exp_buf"][b] = s_e
                    wpv = s_e
                    if jd >= 0:
                        emv = ev[:, :, C0:C0 + 128]
                        wpv = S.op("vector", TT(emv, emv, tri_b, ALU.mult), waits=[s_e])
                    start = kb == 0
                    stop = kb == nkt - 1

                    def mk_pv(pva_s=pva_s, pvb_s=pvb_s, C0=C0, kb=kb, j=j, b=b,
                              start=start, stop=stop, wpv=wpv, sset=sset,
                              pv_stop=pv_stop, first_pv=first_pv):
                        def emit():
                            w = [wpv, st["v_ready"][kb // 4]]
                            if start:
                                w.append(st["pv_free"][sset])
                                if first_pv[sset]:
                                    first_pv[sset] = False
                                    w += [st["bank_free"][4 + 2 * sset],
                                          st["bank_free"][5 + 2 * sset]]
                            S.op("tensor", MM(
                                pva_s[0:65, C0:512],
                                v_all[:, kb * (16 * VW) + (2 * j) * VW: kb * (16 * VW) + (2 * j) * VW + VW],
                                ebuf[:, b * 1024 + C0: b * 1024 + 512], start, stop),
                                waits=w)
                            t2 = S.op("tensor", MM(
                                pvb_s[0:65, C0:512],
                                v_all[:, kb * (16 * VW) + (2 * j + 1) * VW: kb * (16 * VW) + (2 * j + 1) * VW + VW],
                                ebuf[:, b * 1024 + 512 + C0: b * 1024 + 1024], start, stop))
                            pv_stop[0] = t2
                        return emit
                    st["pend_pv"] = mk_pv()
                    st["pat"] += 1

                def mk_epi(pva_s=pva_s, pvb_s=pvb_s, j=j, sset=sset, pout=pout,
                           pv_stop=pv_stop, g=g):
                    def emit():
                        ocol = pout * SEQ + j * 512
                        if g <= 2:
                            s_la = S.op("scalar", ACT(rbuf[0:1, 0:512], pva_s[64:65, :],
                                                      AF.Ln),
                                        waits=[pv_stop[0], st["rbuf_free"]])
                            S.op("scalar", ACT(rbuf[0:1, 512:1024], pvb_s[64:65, :],
                                               AF.Ln))
                            v_rb = S.op("scalar", ACT(rbuf[0:1, 0:1024], rbuf[0:1, 0:1024],
                                                      AF.Exp, scale=-1.0))
                            t_ba = S.op("tensor", MM(pva_s[64:128, :], ones_r[0:1, 0:64],
                                                     rbuf[0:1, 0:512], True, True),
                                        waits=[v_rb])
                            t_bb = S.op("tensor", MM(pvb_s[64:128, :], ones_r[0:1, 0:64],
                                                     rbuf[0:1, 512:1024], True, True))
                            st["rbuf_free"] = t_bb
                            S.op("vector", CP(rb_sb[:, 0:512], pva_s[64:128, :]),
                                 waits=[t_ba, st["rb_free"]])
                            S.op("vector", CP(rb_sb[:, 512:1024], pvb_s[64:128, :]),
                                 waits=[t_bb])
                            v_ma = S.op("vector", TT(qc2[0:64, ocol:ocol + 512],
                                                     pva_s[0:64, :], rb_sb[:, 0:512],
                                                     ALU.mult))
                            v_mb = S.op("vector", TT(qc2[64:128, ocol:ocol + 512],
                                                     pvb_s[0:64, :], rb_sb[:, 512:1024],
                                                     ALU.mult))
                        else:
                            v_sa = S.op("vector", CP(sq[0:65, 0:512], pva_s[0:65, :]),
                                        waits=[pv_stop[0], st["sq_war"][0], st["sq_war"][1],
                                               st["sq_scalar_user"]])
                            v_sb = S.op("vector", CP(sq[0:65, 512:1024], pvb_s[0:65, :]))
                            st["pv_free"][sset] = v_sb
                            S.op("vector", RC(rbuf[0:1, 0:512], sq[64:65, 0:512]),
                                 waits=[st["rbuf_free"]])
                            v_rb = S.op("vector", RC(rbuf[0:1, 512:1024], sq[64:65, 512:1024]))
                            d1 = S.op("sync", DMA(SCR_R, rbuf[0:1, 0:1024]),
                                      waits=[v_rb, st["scr_free"]], sem="dbc", amt=16)
                            d2 = S.op("sync", DMA(
                                rb_sb[:, :],
                                SCR_R[0:1, 0:1024].partition_broadcast(64)[:, 0, :]),
                                waits=[d1, st["rb_free"]], sem="dbc", amt=16)
                            st["rbuf_free"] = d1
                            st["scr_free"] = d2
                            v_ma = S.op("vector", TT(qc2[0:64, ocol:ocol + 512],
                                                     sq[0:64, 0:512], rb_sb[:, 0:512],
                                                     ALU.mult),
                                        waits=[d2])
                            v_mb = S.op("vector", TT(qc2[64:128, ocol:ocol + 512],
                                                     sq[0:64, 512:1024], rb_sb[:, 512:1024],
                                                     ALU.mult))
                        st["rb_free"] = v_mb
                        st["ctx_ready"][j] = v_mb
                        if g <= 2:
                            st["pv_free"][sset] = v_mb
                    return emit
                prev = st["pend_epi"]
                assert prev is None
                st["pend_epi"] = mk_epi()
        flush_pv()
        owin_war = st["owin_war"]
        stops = [None] * 8
        for c in range(NCH):
            slot = c % 2
            wv = owin[:, slot * 1024:(slot + 1) * 1024]
            d = S.op("scalar", DMA(wv, OW_S[c * 128:(c + 1) * 128, :]),
                     waits=[ow_ready, owin_war[slot]], sem="dw%d" % slot, amt=16)
            for o in range(8):
                w = [d, st["ctx_ready"][c]]
                if c == 0:
                    w += [st["exp_buf"][0], st["exp_buf"][1],
                          st["pv_free"][0], st["pv_free"][1]]
                stops[o] = S.op("tensor", MM(banks[o], wv[:, o * 128:(o + 1) * 128],
                                             qc2[:, 1 * SEQ + c * 512: 1 * SEQ + (c + 1) * 512],
                                             c == 0, c == 7), waits=w)
            owin_war[slot] = stops[7]
        x2_ready = []
        for o in range(8):
            xo = xs[:, o * 512:(o + 1) * 512]
            v = S.op("vector", STT(xo, banks[o], outb[:, o:o + 1], xo, ALU.add, ALU.add),
                     waits=[stops[o]])
            st["bank_free"][o] = v
            x2_ready.append(v)
        st["x2_ready"] = x2_ready

    def phase_C(g):
        norm = emit_ln(None, g, None, chunk_ready=st["x2_ready"])
        wwin_war = st["wwin_war"]
        h_ready = [None] * 16
        for s in range(2):
            stops = [None] * 8
            for c in range(NCH):
                slot = c % 2
                wv = wwin[:, slot * 1024:(slot + 1) * 1024]
                d = S.op("scalar", DMA(wv, W1_S[c * 128:(c + 1) * 128, s * 1024:(s + 1) * 1024]),
                         waits=[w1_ready, wwin_war[slot]], sem="dw%d" % slot, amt=16)
                for j in range(8):
                    w = [d, norm[c]]
                    if c == 0:
                        w.append(st["bank_free"][j])
                    stops[j] = S.op("tensor", MM(banks[j], wv[:, j * 128:(j + 1) * 128],
                                                 lnb[:, c * 512:(c + 1) * 512],
                                                 c == 0, c == 7), waits=w)
                wwin_war[slot] = stops[7]
            for j in range(8):
                t = s * 8 + j
                e = S.op("vector", (lambda ee, o=qc2[:, t * 512:(t + 1) * 512],
                                    i=banks[j], bb=b1t[:, t:t + 1]:
                                    ee.tensor_scalar(o, i, bb, 0.0, op0=ALU.add,
                                                     op1=ALU.max)),
                         waits=[stops[j]])
                h_ready[t] = e
                st["bank_free"][j] = e
        st["lnb_free"] = stops[7]
        owin_war = st["owin_war"]
        stops = [None] * 8
        for hc in range(16):
            slot = hc % 2
            wv = owin[:, slot * 1024:(slot + 1) * 1024]
            d = S.op("scalar", DMA(wv, W2_S[hc * 128:(hc + 1) * 128, :]),
                     waits=[w2_ready, owin_war[slot]], sem="dw%d" % slot, amt=16)
            for o in range(8):
                w = [d, h_ready[hc]]
                if hc == 0:
                    w.append(st["bank_free"][o])
                stops[o] = S.op("tensor", MM(banks[o], wv[:, o * 128:(o + 1) * 128],
                                             qc2[:, hc * 512:(hc + 1) * 512],
                                             hc == 0, hc == 15), waits=w)
            owin_war[slot] = stops[7]
        st["qc2_free"] = stops[7]
        last_store = None
        for o in range(8):
            xo = xs[:, o * 512:(o + 1) * 512]
            v = S.op("vector", STT(xo, banks[o], b2t[:, o:o + 1], xo, ALU.add, ALU.add),
                     waits=[stops[o]])
            st["bank_free"][o] = v
            last_store = S.op("sync", DMA(OUT[o * 128:(o + 1) * 128, g * GT:(g + 1) * GT], xo),
                              waits=[v], sem="dout", amt=16)
        st["x_free"] = last_store

    for g in range(NG):
        phase_A(g)
        phase_B(g)
        phase_C(g)
    S.wait_only("sync", [("dout", S.cnt["dout"])])

    with es:
        with nc.Block() as block:
            def runner(eng, name):
                lastw = {}
                for waits, emit, sem, amt in S.items[name]:
                    for (s, v) in waits:
                        if s == name:
                            continue
                        if lastw.get(s, 0) < v:
                            eng.wait_ge(sems[s], v)
                            lastw[s] = v
                    if emit is not None:
                        inst = emit(eng)
                        if sem is not None:
                            inst.then_inc(sems[sem], amt)

            @block.tensor
            def _(t):
                runner(t, "tensor")

            @block.scalar
            def _(s):
                runner(s, "scalar")

            @block.vector
            def _(v):
                runner(v, "vector")

            @block.gpsimd
            def _(gp):
                runner(gp, "gpsimd")

            @block.sync
            def _(sy):
                runner(sy, "sync")
    return nc


_CACHE = {}


def kernel(L0_attn_x, L0_attn_Wqkv_w, L0_attn_Wqkv_b, L0_attn_out_w, L0_attn_out_b,
           L0_ffn_w_gate, L0_ffn_w_noise, L0_ffn_W1, L0_ffn_b1, L0_ffn_W2, L0_ffn_b2):
    del L0_ffn_w_gate, L0_ffn_w_noise
    x = np.asarray(L0_attn_x, dtype=np.float32)
    wqkvT = np.ascontiguousarray(np.asarray(L0_attn_Wqkv_w, np.float32).T)
    qkv_b = np.asarray(L0_attn_Wqkv_b, np.float32)
    qb = np.ascontiguousarray(qkv_b.reshape(24, 128).T)
    vbr = np.ascontiguousarray(qkv_b[2048:3072].reshape(1, 1024))
    owT = np.ascontiguousarray(np.asarray(L0_attn_out_w, np.float32).T)
    ob = np.ascontiguousarray(np.asarray(L0_attn_out_b, np.float32).reshape(8, 128).T)
    w1 = np.ascontiguousarray(np.asarray(L0_ffn_W1, np.float32)[1])
    b1 = np.ascontiguousarray(np.asarray(L0_ffn_b1, np.float32)[1].reshape(16, 128).T)
    w2 = np.ascontiguousarray(np.asarray(L0_ffn_W2, np.float32)[1])
    b2 = np.ascontiguousarray(np.asarray(L0_ffn_b2, np.float32)[1].reshape(8, 128).T)
    tri = np.ascontiguousarray(np.triu(np.ones((128, 128), np.float32)))

    if "nc" not in _CACHE:
        _CACHE["nc"] = build_nc()
    nc = _CACHE["nc"]

    in_maps = []
    for b in range(8):
        in_maps.append({
            "xt": np.ascontiguousarray(x[b].T),
            "wq": wqkvT, "qb": qb, "vbr": vbr, "ow": owT, "ob": ob,
            "w1": w1, "b1": b1, "w2": w2, "b2": b2, "tri": tri,
        })
    res = run_bass_kernel_spmd(nc, in_maps, core_ids=list(range(8)))
    out = np.empty((8, SEQ, DM), np.float32)
    for b in range(8):
        out[b] = np.asarray(res.results[b]["outT"], np.float32).T
    return out


# version 17 (final)
# speedup vs baseline: 1.0074x; 1.0074x over previous
import numpy as np
from contextlib import ExitStack

import concourse.bass as bass
import concourse.mybir as mybir
from concourse.bass_utils import run_bass_kernel_spmd

F32 = mybir.dt.float32
BF16 = mybir.dt.bfloat16
AF = mybir.ActivationFunctionType
ALU = mybir.AluOpType

SEQ = 4096
DM = 1024
NG = 8
GT = 512
NCH = 8
NKT = 32
VW = 65

ENGS = ("tensor", "scalar", "vector", "gpsimd", "sync")


class Sched:

    def __init__(self):
        self.items = {e: [] for e in ENGS}
        self.cnt = {}
        self.pending = {e: [] for e in ENGS}

    def op(self, eng, emit, waits=(), sem=None, amt=1):
        if sem is None:
            sem = eng
        self.cnt.setdefault(sem, 0)
        w = [x for x in self.pending[eng] if x is not None]
        self.pending[eng] = []
        w += [x for x in waits if x is not None]
        self.items[eng].append((w, emit, sem, amt))
        self.cnt[sem] += amt
        return (sem, self.cnt[sem])

    def wait_only(self, eng, waits):
        self.items[eng].append(([x for x in waits if x is not None], None, None, 0))

    def barrier(self, engs=ENGS):
        marks = [(s, c) for s, c in self.cnt.items() if c > 0]
        for e in engs:
            self.pending[e] = marks[:]


def build_nc():
    nc = bass.Bass("TRN2", target_bir_lowering=False)

    XT = nc.dram_tensor("xt", [DM, SEQ], F32, kind="ExternalInput").ap()
    WQ = nc.dram_tensor("wq", [DM, 3 * DM], F32, kind="ExternalInput").ap()
    QB = nc.dram_tensor("qb", [128, 24], F32, kind="ExternalInput").ap()
    VBR = nc.dram_tensor("vbr", [1, DM], F32, kind="ExternalInput").ap()
    OW = nc.dram_tensor("ow", [DM, DM], F32, kind="ExternalInput").ap()
    OB = nc.dram_tensor("ob", [128, 8], F32, kind="ExternalInput").ap()
    W1 = nc.dram_tensor("w1", [DM, 2048], F32, kind="ExternalInput").ap()
    B1 = nc.dram_tensor("b1", [128, 16], F32, kind="ExternalInput").ap()
    W2 = nc.dram_tensor("w2", [2048, DM], F32, kind="ExternalInput").ap()
    B2 = nc.dram_tensor("b2", [128, 8], F32, kind="ExternalInput").ap()
    TRI = nc.dram_tensor("tri", [128, 128], F32, kind="ExternalInput").ap()
    OUT = nc.dram_tensor("outT", [DM, SEQ], F32, kind="ExternalOutput").ap()

    WQ_S = nc.dram_tensor("wq_s", [DM, 3 * DM], BF16, kind="Internal").ap()
    OW_S = nc.dram_tensor("ow_s", [DM, DM], BF16, kind="Internal").ap()
    W1_S = nc.dram_tensor("w1_s", [DM, 2048], BF16, kind="Internal").ap()
    W2_S = nc.dram_tensor("w2_s", [2048, DM], BF16, kind="Internal").ap()
    SCR_R = nc.dram_tensor("scr_r", [1, 1024], F32, kind="Internal").ap()

    es = ExitStack()
    sb = lambda name, shape, dt: es.enter_context(nc.sbuf_tensor(name, shape, dt))[:]
    ps = lambda name, shape: es.enter_context(nc.psum_tensor(name, shape, F32))[:]

    k_all = sb("k_all", [128, 8 * SEQ], BF16)
    v_all = sb("v_all", [128, NKT * 16 * VW], BF16)
    qc2 = sb("qc2", [128, 2 * SEQ], BF16)
    xs = sb("xs", [128, NCH * GT], F32)
    lnb = sb("lnb", [128, NCH * GT], BF16)
    wwin = sb("wwin", [128, 2 * 1024], BF16)
    owin = sb("owin", [128, 2 * 1024], BF16)
    ebuf = sb("ebuf", [128, 2 * 1024], BF16)
    rb_sb = sb("rb_sb", [64, 1024], F32)
    sq = sb("sq", [128, 2 * 512], F32)
    tmp_ln = sb("tmp_ln", [128, 512], F32)
    vbias = sb("vbias", [128, DM], F32)
    tri = sb("tri_sb", [128, 128], BF16)
    qkvb = sb("qkvb", [128, 24], F32)
    kbias = sb("kbias", [128, 8], F32)
    outb = sb("outb", [128, 8], F32)
    b1t = sb("b1t", [128, 16], F32)
    b2t = sb("b2t", [128, 8], F32)
    stats = sb("stats", [1, 3 * 512], F32)
    rbuf = sb("rbuf", [1, 1024], F32)
    ones_c = sb("ones_c", [128, 1], F32)
    ones_r = sb("ones_r", [1, 128], F32)
    eps_t = sb("eps_t", [1, 1], F32)

    sc0 = ps("sc0", [128, 1024])
    sc1 = ps("sc1", [128, 1024])
    pva0 = ps("pva0", [128, 512])
    pvb0 = ps("pvb0", [128, 512])
    pva1 = ps("pva1", [128, 512])
    pvb1 = ps("pvb1", [128, 512])
    banks = [sc0[:, 0:512], sc0[:, 512:1024], sc1[:, 0:512], sc1[:, 512:1024],
             pva0, pvb0, pva1, pvb1]

    sems = {}
    for s in ("tensor", "scalar", "vector", "gpsimd", "dx", "dw0", "dw1", "dg", "dgq",
              "dout", "dbc"):
        sems[s] = es.enter_context(nc.semaphore(s + "_sem"))

    S = Sched()
    st = {
        "bank_free": [None] * 8,
        "q_ready": [None] * 8,
        "ctx_ready": [None] * 8,
        "pv_free": [None, None],
        "exp_buf": [None, None],
        "pat": 0,
        "x_free": None,
        "x2_ready": None,
        "pend_pv": None,
        "pend_epi": None,
        "rbuf_free": None,
        "rb_free": None,
        "scr_free": None,
        "k_ready": [None] * NG,
        "v_ready": [None] * NG,
        "lnb_free": None,
        "qc2_free": None,
        "ln_free": None,
        "stats_free": None,
        "sq_war": [None, None],
        "sq_scalar_user": None,
        "wwin_war": [None, None],
        "owin_war": [None, None],
    }

    def MM(out, lhsT, rhs, start, stop):
        return lambda e, o=out, l=lhsT, r=rhs, s1=start, s2=stop: e.matmul(
            o, l, r, start=s1, stop=s2)

    def ACT(out, in_, func, bias=None, scale=None):
        kw = {}
        if bias is not None:
            kw["bias"] = bias
        if scale is not None:
            kw["scale"] = scale
        return lambda e, o=out, i=in_, f=func, k=dict(kw): e.activation(out=o, in_=i, func=f, **k)

    def TT(out, a, b, op):
        return lambda e, o=out, x=a, y=b, p=op: e.tensor_tensor(o, x, y, op=p)

    def STT(out, in0, scalar, in1, op0, op1):
        return lambda e, o=out, a=in0, s=scalar, b=in1, p0=op0, p1=op1: \
            e.scalar_tensor_tensor(o, a, s, b, op0=p0, op1=p1)

    def DMA(out, in_):
        return lambda e, o=out, i=in_: e.dma_start(out=o, in_=i)

    def CP(out, in_):
        return lambda e, o=out, i=in_: e.tensor_copy(o, i)

    def RC(out, in_):
        return lambda e, o=out, i=in_: e.reciprocal(o, i)

    def MS(ap, val):
        return lambda e, a=ap, v=val: e.memset(a, v)

    def MUL(out, in_, m):
        return lambda e, o=out, i=in_, v=m: e.mul(o, i, v)

    pair3 = lambda ap: ap.rearrange("p (o c) -> p o c", c=512)

    def dmy(out, waits, extra=()):
        S.op("tensor", MM(out, ones_r[0:1, 0:64], ones_r[0:1, 64:128], True, True),
             waits=list(waits) + list(extra))

    wq_ready = S.op("gpsimd", DMA(WQ_S, WQ), sem="dgq", amt=16)
    S.op("gpsimd", DMA(tri, TRI), sem="dg", amt=16)
    S.op("gpsimd", DMA(OW_S, OW), sem="dg", amt=16)
    S.op("gpsimd", DMA(W1_S, W1), sem="dg", amt=16)
    all_scratch = S.op("gpsimd", DMA(W2_S, W2), sem="dg", amt=16)
    tri_ld = ow_ready = w1_ready = w2_ready = all_scratch

    S.op("sync", DMA(qkvb, QB), sem="dx", amt=16)
    S.op("sync", DMA(rbuf[0:1, 0:1024], VBR), sem="dx", amt=16)
    S.op("sync", DMA(outb, OB), sem="dx", amt=16)
    S.op("sync", DMA(b1t, B1), sem="dx", amt=16)
    d_bias_all = S.op("sync", DMA(b2t, B2), sem="dx", amt=16)
    d_qb = d_vbr = d_bias_all

    v_oc = S.op("vector", MS(ones_c, 1.0 / DM))
    S.op("vector", MS(eps_t, 1e-5))
    v_or = S.op("vector", MS(ones_r, 1.0))
    vones_view = v_all.rearrange("p (t h c) -> p t h c", h=16, c=VW)[:, :, :, 64:65]
    v_vo = S.op("vector", MS(vones_view, 1.0))


    t_vb1 = S.op("tensor", MM(pva0, ones_r[0:1, :], rbuf[0:1, 0:512], True, True),
                 waits=[v_or, d_vbr])
    t_vb2 = S.op("tensor", MM(pvb0, ones_r[0:1, :], rbuf[0:1, 512:1024], True, True))
    S.op("vector", CP(vbias[:, 0:512], pva0), waits=[t_vb1])
    v_vb = S.op("vector", CP(vbias[:, 512:1024], pvb0), waits=[t_vb2])
    S.barrier()

    def emit_ln(src_dram, g, x_wait, chunk_ready=None):
        if src_dram is None:
            xld = list(chunk_ready)
        else:
            last = None
            for c in range(NCH):
                last = S.op("sync", DMA(xs[:, c * 512:(c + 1) * 512],
                                        src_dram[c * 128:(c + 1) * 128, g * GT:(g + 1) * GT]),
                            waits=[x_wait], sem="dx", amt=16)
                if c % 2 == 1:
                    dmy(pva1[0:64, 0:64], [last], [st["bank_free"][6]])
            xld = [last] * NCH
        t_sum = t_sq = None
        for c in range(NCH):
            xc = xs[:, c * 512:(c + 1) * 512]
            sqc = sq[:, (c % 2) * 512:((c % 2) + 1) * 512]
            v_s = S.op("vector", TT(sqc, xc, xc, ALU.mult),
                       waits=[xld[c], st["sq_war"][c % 2], st["sq_scalar_user"]])
            w = [xld[c]]
            if c == 0:
                w += [st["bank_free"][4], st["bank_free"][5]]
            t_sum = S.op("tensor", MM(pva0[0:1, :], ones_c, xc, c == 0, c == 7),
                         waits=w)
            t_sq = S.op("tensor", MM(pvb0[0:1, :], ones_c, sqc, c == 0, c == 7),
                        waits=[v_s])
            st["sq_war"][c % 2] = t_sq
        a_sq = S.op("scalar", (lambda e, o=stats[0:1, 1024:1536], i=pva0[0:1, :]:
                               e.activation(out=o, in_=i, func=AF.Square)),
                    waits=[t_sum, t_sq, st["stats_free"], st["ln_free"]])
        v_vr = S.op("vector", TT(stats[0:1, 512:1024], pvb0[0:1, :],
                                 stats[0:1, 1024:1536], ALU.subtract),
                    waits=[a_sq, st["stats_free"]])
        dmy(pva1[0:64, 0:64], [a_sq], [st["bank_free"][6]])
        a_ln = S.op("scalar", ACT(stats[0:1, 1024:1536], stats[0:1, 512:1024],
                                  AF.Ln, bias=eps_t[0:1, 0:1]), waits=[v_vr])
        a_rs = S.op("scalar", ACT(stats[0:1, 512:1024], stats[0:1, 1024:1536],
                                  AF.Exp, scale=-0.5))
        dmy(pva1[0:64, 0:64], [a_ln])
        v_mr = S.op("vector", TT(stats[0:1, 0:512], pva0[0:1, :],
                                 stats[0:1, 512:1024], ALU.mult),
                    waits=[a_rs, st["stats_free"]])
        a_m = a_q = a_rs
        t_b1 = S.op("tensor", MM(pva1, ones_r[0:1, :], stats[0:1, 512:1024], True, True),
                    waits=[a_rs, st["bank_free"][6]])
        t_b2 = S.op("tensor", MM(pvb1, ones_r[0:1, :], stats[0:1, 0:512], True, True),
                    waits=[v_mr, st["bank_free"][7]])
        norm = []
        for c in range(NCH):
            xc = xs[:, c * 512:(c + 1) * 512]
            S.op("vector", TT(tmp_ln, xc, pva1, ALU.mult),
                 waits=[t_b1, st["lnb_free"] if c == 0 else None])
            v2 = S.op("vector", TT(lnb[:, c * 512:(c + 1) * 512], tmp_ln, pvb1,
                                   ALU.subtract), waits=[t_b2])
            norm.append(v2)
        st["bank_free"][4] = v_mr
        st["bank_free"][5] = v_mr
        st["bank_free"][6] = norm[-1]
        st["bank_free"][7] = norm[-1]
        st["stats_free"] = t_b2
        st["ln_free"] = v_mr
        return norm

    def phase_A(g):
        norm = emit_ln(XT, g, st["x_free"])
        wwin_war = st["wwin_war"]
        for b in range(3):
            stops = [None] * 8
            for c in range(NCH):
                slot = c % 2
                wv = wwin[:, slot * 1024:(slot + 1) * 1024]
                d = S.op("scalar", DMA(wv, WQ_S[c * 128:(c + 1) * 128, b * 1024:(b + 1) * 1024]),
                         waits=[wq_ready, wwin_war[slot]], sem="dw%d" % slot, amt=16)
                for j in range(8):
                    if b < 2:
                        lhsT = wv[:, j * 128:(j + 1) * 128]
                        rhs = lnb[:, c * 512:(c + 1) * 512]
                    else:
                        tc, half = j // 2, j % 2
                        lhsT = lnb[:, c * 512 + tc * 128: c * 512 + (tc + 1) * 128]
                        rhs = wv[:, half * 512:(half + 1) * 512]
                    w = [d, norm[c]]
                    if c == 0:
                        w.append(st["bank_free"][j])
                    stops[j] = S.op("tensor", MM(banks[j], lhsT, rhs, c == 0, c == 7),
                                    waits=w)
                wwin_war[slot] = stops[7]
            for j in range(8):
                if b == 0:
                    e = S.op("vector", (lambda ee, o=qc2[:, j * 512:(j + 1) * 512],
                                        i=banks[j], bb=qkvb[:, j:j + 1]:
                                        ee.tensor_scalar(o, i, bb, None, op0=ALU.add)),
                             waits=[stops[j], st["qc2_free"] if j == 0 else None])
                    st["q_ready"][j] = e
                elif b == 1:
                    e = S.op("vector", (lambda ee, o=k_all[:, j * SEQ + g * 512: j * SEQ + (g + 1) * 512],
                                        i=banks[j], bb=qkvb[:, 8 + j:9 + j]:
                                        ee.tensor_scalar(o, i, bb, 0.125, op0=ALU.add,
                                                         op1=ALU.mult)),
                             waits=[stops[j]])
                    st["k_ready"][g] = e
                else:
                    tc, half = j // 2, j % 2
                    base = (g * 4 + tc) * (16 * VW) + half * 8 * VW
                    dst = v_all[:, base: base + 8 * VW].rearrange(
                        "p (h c) -> p h c", c=VW)[:, :, 0:64]
                    srcv = banks[j].rearrange("p (h c) -> p h c", c=64)
                    vbv = vbias[:, half * 512:(half + 1) * 512].rearrange(
                        "p (h c) -> p h c", c=64)
                    e = S.op("vector", TT(dst, srcv, vbv, ALU.add),
                             waits=[stops[j], v_vo if g == 0 else None])
                st["bank_free"][j] = e
                if b == 2:
                    st["v_ready"][g] = e
        st["lnb_free"] = stops[7]

    tri_b = tri.rearrange("p (o c) -> p o c", c=128).broadcast_to([128, 2, 128])

    def flush_pv():
        if st["pend_pv"] is not None:
            f = st["pend_pv"]
            st["pend_pv"] = None
            f()
        if st["pend_epi"] is not None:
            f = st["pend_epi"]
            st["pend_epi"] = None
            f()

    def phase_B(g):
        nkt = 4 * g + 4
        first_sc = [True, True]
        first_pv = [True, True]
        for r in range(3):
            pin = r % 2
            pout = 1 - pin
            for j in range(8):
                sset = j % 2
                pva_s = pva0 if sset == 0 else pva1
                pvb_s = pvb0 if sset == 0 else pvb1
                pv_stop = [None]
                for kb in range(nkt):
                    i = st["pat"]
                    b = i % 2
                    scb = sc0 if b == 0 else sc1
                    jd = kb - 4 * g
                    C0 = 128 * jd if jd >= 0 else 0
                    qcol = pin * SEQ + j * 512
                    w_rhs = st["ctx_ready"][j] if r > 0 else st["q_ready"][j]
                    w_sc = [st["exp_buf"][b], w_rhs, st["k_ready"][kb // 4]]
                    if first_sc[b]:
                        first_sc[b] = False
                        w_sc += [st["bank_free"][2 * b], st["bank_free"][2 * b + 1]]
                    def MMT(out, lhsT, rhs, tp):
                        return lambda e, o=out, l=lhsT, r=rhs, t=tp: e.matmul(
                            o, l, r, start=True, stop=True, tile_position=t)
                    t_e = S.op("tensor", MMT(
                        scb[:, C0:512],
                        k_all[0:64, j * SEQ + kb * 128: j * SEQ + (kb + 1) * 128],
                        qc2[0:64, qcol + C0: qcol + 512], (0, 0)),
                        waits=w_sc)
                    t_o = S.op("tensor", MMT(
                        scb[:, 512 + C0:1024],
                        k_all[64:128, j * SEQ + kb * 128: j * SEQ + (kb + 1) * 128],
                        qc2[64:128, qcol + C0: qcol + 512], (64, 0)))
                    flush_pv()
                    ev = pair3(ebuf[:, b * 1024:(b + 1) * 1024])
                    s_e = S.op("scalar", ACT(ev[:, :, C0:512],
                                             pair3(scb)[:, :, C0:512], AF.Exp),
                               waits=[t_o])
                    st["exp_buf"][b] = s_e
                    wpv = s_e
                    if jd >= 0:
                        emv = ev[:, :, C0:C0 + 128]
                        wpv = S.op("vector", TT(emv, emv, tri_b, ALU.mult), waits=[s_e])
                    start = kb == 0
                    stop = kb == nkt - 1

                    def mk_pv(pva_s=pva_s, pvb_s=pvb_s, C0=C0, kb=kb, j=j, b=b,
                              start=start, stop=stop, wpv=wpv, sset=sset,
                              pv_stop=pv_stop, first_pv=first_pv):
                        def emit():
                            w = [wpv, st["v_ready"][kb // 4]]
                            if start:
                                w.append(st["pv_free"][sset])
                                if first_pv[sset]:
                                    first_pv[sset] = False
                                    w += [st["bank_free"][4 + 2 * sset],
                                          st["bank_free"][5 + 2 * sset]]
                            S.op("tensor", MM(
                                pva_s[0:65, C0:512],
                                v_all[:, kb * (16 * VW) + (2 * j) * VW: kb * (16 * VW) + (2 * j) * VW + VW],
                                ebuf[:, b * 1024 + C0: b * 1024 + 512], start, stop),
                                waits=w)
                            t2 = S.op("tensor", MM(
                                pvb_s[0:65, C0:512],
                                v_all[:, kb * (16 * VW) + (2 * j + 1) * VW: kb * (16 * VW) + (2 * j + 1) * VW + VW],
                                ebuf[:, b * 1024 + 512 + C0: b * 1024 + 1024], start, stop))
                            pv_stop[0] = t2
                        return emit
                    st["pend_pv"] = mk_pv()
                    st["pat"] += 1

                def mk_epi(pva_s=pva_s, pvb_s=pvb_s, j=j, sset=sset, pout=pout,
                           pv_stop=pv_stop, g=g):
                    def emit():
                        ocol = pout * SEQ + j * 512
                        if g <= 2:
                            s_la = S.op("scalar", ACT(rbuf[0:1, 0:512], pva_s[64:65, :],
                                                      AF.Ln),
                                        waits=[pv_stop[0], st["rbuf_free"]])
                            S.op("scalar", ACT(rbuf[0:1, 512:1024], pvb_s[64:65, :],
                                               AF.Ln))
                            v_rb = S.op("scalar", ACT(rbuf[0:1, 0:1024], rbuf[0:1, 0:1024],
                                                      AF.Exp, scale=-1.0))
                            t_ba = S.op("tensor", MM(pva_s[64:128, :], ones_r[0:1, 0:64],
                                                     rbuf[0:1, 0:512], True, True),
                                        waits=[v_rb])
                            t_bb = S.op("tensor", MM(pvb_s[64:128, :], ones_r[0:1, 0:64],
                                                     rbuf[0:1, 512:1024], True, True))
                            st["rbuf_free"] = t_bb
                            S.op("vector", CP(rb_sb[:, 0:512], pva_s[64:128, :]),
                                 waits=[t_ba, st["rb_free"]])
                            S.op("vector", CP(rb_sb[:, 512:1024], pvb_s[64:128, :]),
                                 waits=[t_bb])
                            v_ma = S.op("vector", TT(qc2[0:64, ocol:ocol + 512],
                                                     pva_s[0:64, :], rb_sb[:, 0:512],
                                                     ALU.mult))
                            v_mb = S.op("vector", TT(qc2[64:128, ocol:ocol + 512],
                                                     pvb_s[0:64, :], rb_sb[:, 512:1024],
                                                     ALU.mult))
                        else:
                            v_sa = S.op("vector", CP(sq[0:65, 0:512], pva_s[0:65, :]),
                                        waits=[pv_stop[0], st["sq_war"][0], st["sq_war"][1],
                                               st["sq_scalar_user"]])
                            v_sb = S.op("vector", CP(sq[0:65, 512:1024], pvb_s[0:65, :]))
                            st["pv_free"][sset] = v_sb
                            S.op("vector", RC(rbuf[0:1, 0:512], sq[64:65, 0:512]),
                                 waits=[st["rbuf_free"]])
                            v_rb = S.op("vector", RC(rbuf[0:1, 512:1024], sq[64:65, 512:1024]))
                            d1 = S.op("sync", DMA(SCR_R, rbuf[0:1, 0:1024]),
                                      waits=[v_rb, st["scr_free"]], sem="dbc", amt=16)
                            d2 = S.op("sync", DMA(
                                rb_sb[:, :],
                                SCR_R[0:1, 0:1024].partition_broadcast(64)[:, 0, :]),
                                waits=[d1, st["rb_free"]], sem="dbc", amt=16)
                            st["rbuf_free"] = d1
                            st["scr_free"] = d2
                            v_ma = S.op("vector", TT(qc2[0:64, ocol:ocol + 512],
                                                     sq[0:64, 0:512], rb_sb[:, 0:512],
                                                     ALU.mult),
                                        waits=[d2])
                            v_mb = S.op("vector", TT(qc2[64:128, ocol:ocol + 512],
                                                     sq[0:64, 512:1024], rb_sb[:, 512:1024],
                                                     ALU.mult))
                        st["rb_free"] = v_mb
                        st["ctx_ready"][j] = v_mb
                        if g <= 2:
                            st["pv_free"][sset] = v_mb
                    return emit
                prev = st["pend_epi"]
                assert prev is None
                st["pend_epi"] = mk_epi()
        flush_pv()
        owin_war = st["owin_war"]
        stops = [None] * 8
        for c in range(NCH):
            slot = c % 2
            wv = owin[:, slot * 1024:(slot + 1) * 1024]
            d = S.op("scalar", DMA(wv, OW_S[c * 128:(c + 1) * 128, :]),
                     waits=[ow_ready, owin_war[slot]], sem="dw%d" % slot, amt=16)
            for o in range(8):
                w = [d, st["ctx_ready"][c]]
                if c == 0:
                    w += [st["exp_buf"][0], st["exp_buf"][1],
                          st["pv_free"][0], st["pv_free"][1]]
                stops[o] = S.op("tensor", MM(banks[o], wv[:, o * 128:(o + 1) * 128],
                                             qc2[:, 1 * SEQ + c * 512: 1 * SEQ + (c + 1) * 512],
                                             c == 0, c == 7), waits=w)
            owin_war[slot] = stops[7]
        x2_ready = []
        for o in range(8):
            xo = xs[:, o * 512:(o + 1) * 512]
            v = S.op("vector", STT(xo, banks[o], outb[:, o:o + 1], xo, ALU.add, ALU.add),
                     waits=[stops[o]])
            st["bank_free"][o] = v
            x2_ready.append(v)
        st["x2_ready"] = x2_ready

    def phase_C(g):
        norm = emit_ln(None, g, None, chunk_ready=st["x2_ready"])
        wwin_war = st["wwin_war"]
        h_ready = [None] * 16
        for s in range(2):
            stops = [None] * 8
            for c in range(NCH):
                slot = c % 2
                wv = wwin[:, slot * 1024:(slot + 1) * 1024]
                d = S.op("scalar", DMA(wv, W1_S[c * 128:(c + 1) * 128, s * 1024:(s + 1) * 1024]),
                         waits=[w1_ready, wwin_war[slot]], sem="dw%d" % slot, amt=16)
                for j in range(8):
                    w = [d, norm[c]]
                    if c == 0:
                        w.append(st["bank_free"][j])
                    stops[j] = S.op("tensor", MM(banks[j], wv[:, j * 128:(j + 1) * 128],
                                                 lnb[:, c * 512:(c + 1) * 512],
                                                 c == 0, c == 7), waits=w)
                wwin_war[slot] = stops[7]
            for j in range(8):
                t = s * 8 + j
                e = S.op("vector", (lambda ee, o=qc2[:, t * 512:(t + 1) * 512],
                                    i=banks[j], bb=b1t[:, t:t + 1]:
                                    ee.tensor_scalar(o, i, bb, 0.0, op0=ALU.add,
                                                     op1=ALU.max)),
                         waits=[stops[j]])
                h_ready[t] = e
                st["bank_free"][j] = e
        st["lnb_free"] = stops[7]
        owin_war = st["owin_war"]
        stops = [None] * 8
        for hc in range(16):
            slot = hc % 2
            wv = owin[:, slot * 1024:(slot + 1) * 1024]
            d = S.op("scalar", DMA(wv, W2_S[hc * 128:(hc + 1) * 128, :]),
                     waits=[w2_ready, owin_war[slot]], sem="dw%d" % slot, amt=16)
            for o in range(8):
                w = [d, h_ready[hc]]
                if hc == 0:
                    w.append(st["bank_free"][o])
                stops[o] = S.op("tensor", MM(banks[o], wv[:, o * 128:(o + 1) * 128],
                                             qc2[:, hc * 512:(hc + 1) * 512],
                                             hc == 0, hc == 15), waits=w)
            owin_war[slot] = stops[7]
        st["qc2_free"] = stops[7]
        last_store = None
        for o in range(8):
            xo = xs[:, o * 512:(o + 1) * 512]
            v = S.op("vector", STT(xo, banks[o], b2t[:, o:o + 1], xo, ALU.add, ALU.add),
                     waits=[stops[o]])
            st["bank_free"][o] = v
            last_store = S.op("sync", DMA(OUT[o * 128:(o + 1) * 128, g * GT:(g + 1) * GT], xo),
                              waits=[v], sem="dout", amt=16)
        st["x_free"] = last_store

    for g in range(NG):
        phase_A(g)
        phase_B(g)
        phase_C(g)
    S.wait_only("sync", [("dout", S.cnt["dout"])])

    import bisect
    eng_set = {"tensor", "scalar", "vector", "gpsimd"}
    refs = {s: set() for s in eng_set}
    for e in ENGS:
        for waits, emit, sem, amt in S.items[e]:
            for (s, v) in waits:
                if s in refs:
                    refs[s].add(v)
    ms = {s: sorted(refs[s]) for s in eng_set}
    planned = {s: 0 for s in eng_set}
    for e in ENGS:
        new_items = []
        for waits, emit, sem, amt in S.items[e]:
            w2 = [(s, bisect.bisect_right(ms[s], v) if s in eng_set else v)
                  for (s, v) in waits]
            if sem in eng_set:
                planned[sem] += amt
                sem2 = sem if planned[sem] in refs[sem] else None
            else:
                sem2 = sem
            new_items.append((w2, emit, sem2, amt))
        S.items[e] = new_items

    with es:
        with nc.Block() as block:
            def runner(eng, name):
                lastw = {}
                for waits, emit, sem, amt in S.items[name]:
                    for (s, v) in waits:
                        if s == name:
                            continue
                        if lastw.get(s, 0) < v:
                            eng.wait_ge(sems[s], v)
                            lastw[s] = v
                    if emit is not None:
                        inst = emit(eng)
                        if sem is not None:
                            inst.then_inc(sems[sem], amt)

            @block.tensor
            def _(t):
                runner(t, "tensor")

            @block.scalar
            def _(s):
                runner(s, "scalar")

            @block.vector
            def _(v):
                runner(v, "vector")

            @block.gpsimd
            def _(gp):
                runner(gp, "gpsimd")

            @block.sync
            def _(sy):
                runner(sy, "sync")
    return nc


_CACHE = {}


def kernel(L0_attn_x, L0_attn_Wqkv_w, L0_attn_Wqkv_b, L0_attn_out_w, L0_attn_out_b,
           L0_ffn_w_gate, L0_ffn_w_noise, L0_ffn_W1, L0_ffn_b1, L0_ffn_W2, L0_ffn_b2):
    del L0_ffn_w_gate, L0_ffn_w_noise
    x = np.asarray(L0_attn_x, dtype=np.float32)
    wqkvT = np.ascontiguousarray(np.asarray(L0_attn_Wqkv_w, np.float32).T)
    qkv_b = np.asarray(L0_attn_Wqkv_b, np.float32)
    qb = np.ascontiguousarray(qkv_b.reshape(24, 128).T)
    vbr = np.ascontiguousarray(qkv_b[2048:3072].reshape(1, 1024))
    owT = np.ascontiguousarray(np.asarray(L0_attn_out_w, np.float32).T)
    ob = np.ascontiguousarray(np.asarray(L0_attn_out_b, np.float32).reshape(8, 128).T)
    w1 = np.ascontiguousarray(np.asarray(L0_ffn_W1, np.float32)[1])
    b1 = np.ascontiguousarray(np.asarray(L0_ffn_b1, np.float32)[1].reshape(16, 128).T)
    w2 = np.ascontiguousarray(np.asarray(L0_ffn_W2, np.float32)[1])
    b2 = np.ascontiguousarray(np.asarray(L0_ffn_b2, np.float32)[1].reshape(8, 128).T)
    tri = np.ascontiguousarray(np.triu(np.ones((128, 128), np.float32)))

    if "nc" not in _CACHE:
        _CACHE["nc"] = build_nc()
    nc = _CACHE["nc"]

    in_maps = []
    for b in range(8):
        in_maps.append({
            "xt": np.ascontiguousarray(x[b].T),
            "wq": wqkvT, "qb": qb, "vbr": vbr, "ow": owT, "ob": ob,
            "w1": w1, "b1": b1, "w2": w2, "b2": b2, "tri": tri,
        })
    res = run_bass_kernel_spmd(nc, in_maps, core_ids=list(range(8)))
    out = np.empty((8, SEQ, DM), np.float32)
    for b in range(8):
        out[b] = np.asarray(res.results[b]["outT"], np.float32).T
    return out


# version 18
# speedup vs baseline: 1.0122x; 1.0047x over previous
import numpy as np
from contextlib import ExitStack

import concourse.bass as bass
import concourse.mybir as mybir
from concourse.bass_utils import run_bass_kernel_spmd

F32 = mybir.dt.float32
BF16 = mybir.dt.bfloat16
AF = mybir.ActivationFunctionType
ALU = mybir.AluOpType

SEQ = 4096
DM = 1024
NG = 8
GT = 512
NCH = 8
NKT = 32
VW = 65

ENGS = ("tensor", "scalar", "vector", "gpsimd", "sync")


class Sched:

    def __init__(self):
        self.items = {e: [] for e in ENGS}
        self.cnt = {}
        self.pending = {e: [] for e in ENGS}

    def op(self, eng, emit, waits=(), sem=None, amt=1):
        if sem is None:
            sem = eng
        self.cnt.setdefault(sem, 0)
        w = [x for x in self.pending[eng] if x is not None]
        self.pending[eng] = []
        w += [x for x in waits if x is not None]
        self.items[eng].append((w, emit, sem, amt))
        self.cnt[sem] += amt
        return (sem, self.cnt[sem])

    def wait_only(self, eng, waits):
        self.items[eng].append(([x for x in waits if x is not None], None, None, 0))

    def barrier(self, engs=ENGS):
        marks = [(s, c) for s, c in self.cnt.items() if c > 0]
        for e in engs:
            self.pending[e] = marks[:]


def build_nc():
    nc = bass.Bass("TRN2", target_bir_lowering=False)

    XT = nc.dram_tensor("xt", [DM, SEQ], F32, kind="ExternalInput").ap()
    WQ = nc.dram_tensor("wq", [DM, 3 * DM], F32, kind="ExternalInput").ap()
    QB = nc.dram_tensor("qb", [128, 24], F32, kind="ExternalInput").ap()
    VBR = nc.dram_tensor("vbr", [1, DM], F32, kind="ExternalInput").ap()
    OW = nc.dram_tensor("ow", [DM, DM], F32, kind="ExternalInput").ap()
    OB = nc.dram_tensor("ob", [128, 8], F32, kind="ExternalInput").ap()
    W1 = nc.dram_tensor("w1", [DM, 2048], F32, kind="ExternalInput").ap()
    B1 = nc.dram_tensor("b1", [128, 16], F32, kind="ExternalInput").ap()
    W2 = nc.dram_tensor("w2", [2048, DM], F32, kind="ExternalInput").ap()
    B2 = nc.dram_tensor("b2", [128, 8], F32, kind="ExternalInput").ap()
    TRI = nc.dram_tensor("tri", [128, 128], F32, kind="ExternalInput").ap()
    OUT = nc.dram_tensor("outT", [DM, SEQ], F32, kind="ExternalOutput").ap()

    WQ_S = nc.dram_tensor("wq_s", [DM, 3 * DM], BF16, kind="Internal").ap()
    OW_S = nc.dram_tensor("ow_s", [DM, DM], BF16, kind="Internal").ap()
    W1_S = nc.dram_tensor("w1_s", [DM, 2048], BF16, kind="Internal").ap()
    W2_S = nc.dram_tensor("w2_s", [2048, DM], BF16, kind="Internal").ap()
    SCR_R = nc.dram_tensor("scr_r", [1, 1024], F32, kind="Internal").ap()

    es = ExitStack()
    sb = lambda name, shape, dt: es.enter_context(nc.sbuf_tensor(name, shape, dt))[:]
    ps = lambda name, shape: es.enter_context(nc.psum_tensor(name, shape, F32))[:]

    k_all = sb("k_all", [128, 8 * SEQ], BF16)
    v_all = sb("v_all", [128, NKT * 16 * VW], BF16)
    qc2 = sb("qc2", [128, 2 * SEQ], BF16)
    xs = sb("xs", [128, NCH * GT], F32)
    lnb = sb("lnb", [128, NCH * GT], BF16)
    wwin = sb("wwin", [128, 2 * 1024], BF16)
    owin = sb("owin", [128, 2 * 1024], BF16)
    ebuf = sb("ebuf", [128, 2 * 1024], BF16)
    rb_sb = sb("rb_sb", [64, 1024], F32)
    sq = sb("sq", [128, 2 * 512], F32)
    tmp_ln = sb("tmp_ln", [128, 512], F32)
    vbias = sb("vbias", [128, DM], F32)
    tri = sb("tri_sb", [128, 128], BF16)
    qkvb = sb("qkvb", [128, 24], F32)
    kbias = sb("kbias", [128, 8], F32)
    outb = sb("outb", [128, 8], F32)
    b1t = sb("b1t", [128, 16], F32)
    b2t = sb("b2t", [128, 8], F32)
    stats = sb("stats", [1, 3 * 512], F32)
    rbuf = sb("rbuf", [1, 1024], F32)
    ones_c = sb("ones_c", [128, 1], F32)
    ones_r = sb("ones_r", [1, 128], F32)
    eps_t = sb("eps_t", [1, 1], F32)

    sc0 = ps("sc0", [128, 1024])
    sc1 = ps("sc1", [128, 1024])
    pva0 = ps("pva0", [128, 512])
    pvb0 = ps("pvb0", [128, 512])
    pva1 = ps("pva1", [128, 512])
    pvb1 = ps("pvb1", [128, 512])
    banks = [sc0[:, 0:512], sc0[:, 512:1024], sc1[:, 0:512], sc1[:, 512:1024],
             pva0, pvb0, pva1, pvb1]

    sems = {}
    for s in ("tensor", "scalar", "vector", "gpsimd", "dx", "dw0", "dw1", "dg",
              "dgq0", "dgq1", "dgq2", "dout", "dbc"):
        sems[s] = es.enter_context(nc.semaphore(s + "_sem"))

    S = Sched()
    st = {
        "bank_free": [None] * 8,
        "q_ready": [None] * 8,
        "ctx_ready": [None] * 8,
        "pv_free": [None, None],
        "exp_buf": [None, None],
        "pat": 0,
        "x_free": None,
        "x2_ready": None,
        "pend_pv": None,
        "pend_epi": None,
        "rbuf_free": None,
        "rb_free": None,
        "scr_free": None,
        "k_ready": [None] * NG,
        "v_ready": [None] * NG,
        "lnb_free": None,
        "qc2_free": None,
        "ln_free": None,
        "stats_free": None,
        "sq_war": [None, None],
        "sq_scalar_user": None,
        "wwin_war": [None, None],
        "owin_war": [None, None],
    }

    def MM(out, lhsT, rhs, start, stop):
        return lambda e, o=out, l=lhsT, r=rhs, s1=start, s2=stop: e.matmul(
            o, l, r, start=s1, stop=s2)

    def ACT(out, in_, func, bias=None, scale=None):
        kw = {}
        if bias is not None:
            kw["bias"] = bias
        if scale is not None:
            kw["scale"] = scale
        return lambda e, o=out, i=in_, f=func, k=dict(kw): e.activation(out=o, in_=i, func=f, **k)

    def TT(out, a, b, op):
        return lambda e, o=out, x=a, y=b, p=op: e.tensor_tensor(o, x, y, op=p)

    def STT(out, in0, scalar, in1, op0, op1):
        return lambda e, o=out, a=in0, s=scalar, b=in1, p0=op0, p1=op1: \
            e.scalar_tensor_tensor(o, a, s, b, op0=p0, op1=p1)

    def DMA(out, in_):
        return lambda e, o=out, i=in_: e.dma_start(out=o, in_=i)

    def CP(out, in_):
        return lambda e, o=out, i=in_: e.tensor_copy(o, i)

    def RC(out, in_):
        return lambda e, o=out, i=in_: e.reciprocal(o, i)

    def MS(ap, val):
        return lambda e, a=ap, v=val: e.memset(a, v)

    def MUL(out, in_, m):
        return lambda e, o=out, i=in_, v=m: e.mul(o, i, v)

    pair3 = lambda ap: ap.rearrange("p (o c) -> p o c", c=512)

    def dmy(out, waits, extra=()):
        S.op("tensor", MM(out, ones_r[0:1, 0:64], ones_r[0:1, 64:128], True, True),
             waits=list(waits) + list(extra))

    wq_ready3 = [S.op("gpsimd", DMA(WQ_S[:, b * 1024:(b + 1) * 1024],
                                    WQ[:, b * 1024:(b + 1) * 1024]),
                      sem="dgq%d" % b, amt=16) for b in range(3)]
    S.op("gpsimd", DMA(tri, TRI), sem="dg", amt=16)
    S.op("gpsimd", DMA(OW_S, OW), sem="dg", amt=16)
    S.op("gpsimd", DMA(W1_S, W1), sem="dg", amt=16)
    all_scratch = S.op("gpsimd", DMA(W2_S, W2), sem="dg", amt=16)
    tri_ld = ow_ready = w1_ready = w2_ready = all_scratch

    S.op("sync", DMA(qkvb, QB), sem="dx", amt=16)
    S.op("sync", DMA(rbuf[0:1, 0:1024], VBR), sem="dx", amt=16)
    S.op("sync", DMA(outb, OB), sem="dx", amt=16)
    S.op("sync", DMA(b1t, B1), sem="dx", amt=16)
    d_bias_all = S.op("sync", DMA(b2t, B2), sem="dx", amt=16)
    d_qb = d_vbr = d_bias_all

    v_oc = S.op("vector", MS(ones_c, 1.0 / DM))
    S.op("vector", MS(eps_t, 1e-5))
    v_or = S.op("vector", MS(ones_r, 1.0))
    vones_view = v_all.rearrange("p (t h c) -> p t h c", h=16, c=VW)[:, :, :, 64:65]
    v_vo = S.op("vector", MS(vones_view, 1.0))


    t_vb1 = S.op("tensor", MM(pva0, ones_r[0:1, :], rbuf[0:1, 0:512], True, True),
                 waits=[v_or, d_vbr])
    t_vb2 = S.op("tensor", MM(pvb0, ones_r[0:1, :], rbuf[0:1, 512:1024], True, True))
    S.op("vector", CP(vbias[:, 0:512], pva0), waits=[t_vb1])
    v_vb = S.op("vector", CP(vbias[:, 512:1024], pvb0), waits=[t_vb2])
    S.barrier()

    def emit_ln(src_dram, g, x_wait, chunk_ready=None):
        if src_dram is None:
            xld = list(chunk_ready)
        else:
            last = None
            for c in range(NCH):
                last = S.op("sync", DMA(xs[:, c * 512:(c + 1) * 512],
                                        src_dram[c * 128:(c + 1) * 128, g * GT:(g + 1) * GT]),
                            waits=[x_wait], sem="dx", amt=16)
                if c % 2 == 1:
                    dmy(pva1[0:64, 0:64], [last], [st["bank_free"][6]])
            xld = [last] * NCH
        t_sum = t_sq = None
        for c in range(NCH):
            xc = xs[:, c * 512:(c + 1) * 512]
            sqc = sq[:, (c % 2) * 512:((c % 2) + 1) * 512]
            v_s = S.op("vector", TT(sqc, xc, xc, ALU.mult),
                       waits=[xld[c], st["sq_war"][c % 2], st["sq_scalar_user"]])
            w = [xld[c]]
            if c == 0:
                w += [st["bank_free"][4], st["bank_free"][5]]
            t_sum = S.op("tensor", MM(pva0[0:1, :], ones_c, xc, c == 0, c == 7),
                         waits=w)
            t_sq = S.op("tensor", MM(pvb0[0:1, :], ones_c, sqc, c == 0, c == 7),
                        waits=[v_s])
            st["sq_war"][c % 2] = t_sq
        a_sq = S.op("scalar", (lambda e, o=stats[0:1, 1024:1536], i=pva0[0:1, :]:
                               e.activation(out=o, in_=i, func=AF.Square)),
                    waits=[t_sum, t_sq, st["stats_free"], st["ln_free"]])
        v_vr = S.op("vector", TT(stats[0:1, 512:1024], pvb0[0:1, :],
                                 stats[0:1, 1024:1536], ALU.subtract),
                    waits=[a_sq, st["stats_free"]])
        dmy(pva1[0:64, 0:64], [a_sq], [st["bank_free"][6]])
        a_ln = S.op("scalar", ACT(stats[0:1, 1024:1536], stats[0:1, 512:1024],
                                  AF.Ln, bias=eps_t[0:1, 0:1]), waits=[v_vr])
        a_rs = S.op("scalar", ACT(stats[0:1, 512:1024], stats[0:1, 1024:1536],
                                  AF.Exp, scale=-0.5))
        dmy(pva1[0:64, 0:64], [a_ln])
        v_mr = S.op("vector", TT(stats[0:1, 0:512], pva0[0:1, :],
                                 stats[0:1, 512:1024], ALU.mult),
                    waits=[a_rs, st["stats_free"]])
        a_m = a_q = a_rs
        t_b1 = S.op("tensor", MM(pva1, ones_r[0:1, :], stats[0:1, 512:1024], True, True),
                    waits=[a_rs, st["bank_free"][6]])
        t_b2 = S.op("tensor", MM(pvb1, ones_r[0:1, :], stats[0:1, 0:512], True, True),
                    waits=[v_mr, st["bank_free"][7]])
        norm = []
        for c in range(NCH):
            xc = xs[:, c * 512:(c + 1) * 512]
            S.op("vector", TT(tmp_ln, xc, pva1, ALU.mult),
                 waits=[t_b1, st["lnb_free"] if c == 0 else None])
            v2 = S.op("vector", TT(lnb[:, c * 512:(c + 1) * 512], tmp_ln, pvb1,
                                   ALU.subtract), waits=[t_b2])
            norm.append(v2)
        st["bank_free"][4] = v_mr
        st["bank_free"][5] = v_mr
        st["bank_free"][6] = norm[-1]
        st["bank_free"][7] = norm[-1]
        st["stats_free"] = t_b2
        st["ln_free"] = v_mr
        return norm

    def phase_A(g):
        norm = emit_ln(XT, g, st["x_free"])
        wwin_war = st["wwin_war"]
        for b in range(3):
            stops = [None] * 8
            for c in range(NCH):
                slot = c % 2
                wv = wwin[:, slot * 1024:(slot + 1) * 1024]
                d = S.op("scalar", DMA(wv, WQ_S[c * 128:(c + 1) * 128, b * 1024:(b + 1) * 1024]),
                         waits=[wq_ready3[b], wwin_war[slot]], sem="dw%d" % slot, amt=16)
                for j in range(8):
                    if b < 2:
                        lhsT = wv[:, j * 128:(j + 1) * 128]
                        rhs = lnb[:, c * 512:(c + 1) * 512]
                    else:
                        tc, half = j // 2, j % 2
                        lhsT = lnb[:, c * 512 + tc * 128: c * 512 + (tc + 1) * 128]
                        rhs = wv[:, half * 512:(half + 1) * 512]
                    w = [d, norm[c]]
                    if c == 0:
                        w.append(st["bank_free"][j])
                    stops[j] = S.op("tensor", MM(banks[j], lhsT, rhs, c == 0, c == 7),
                                    waits=w)
                wwin_war[slot] = stops[7]
            for j in range(8):
                if b == 0:
                    e = S.op("vector", (lambda ee, o=qc2[:, j * 512:(j + 1) * 512],
                                        i=banks[j], bb=qkvb[:, j:j + 1]:
                                        ee.tensor_scalar(o, i, bb, None, op0=ALU.add)),
                             waits=[stops[j], st["qc2_free"] if j == 0 else None])
                    st["q_ready"][j] = e
                elif b == 1:
                    e = S.op("vector", (lambda ee, o=k_all[:, j * SEQ + g * 512: j * SEQ + (g + 1) * 512],
                                        i=banks[j], bb=qkvb[:, 8 + j:9 + j]:
                                        ee.tensor_scalar(o, i, bb, 0.125, op0=ALU.add,
                                                         op1=ALU.mult)),
                             waits=[stops[j]])
                    st["k_ready"][g] = e
                else:
                    tc, half = j // 2, j % 2
                    base = (g * 4 + tc) * (16 * VW) + half * 8 * VW
                    dst = v_all[:, base: base + 8 * VW].rearrange(
                        "p (h c) -> p h c", c=VW)[:, :, 0:64]
                    srcv = banks[j].rearrange("p (h c) -> p h c", c=64)
                    vbv = vbias[:, half * 512:(half + 1) * 512].rearrange(
                        "p (h c) -> p h c", c=64)
                    e = S.op("vector", TT(dst, srcv, vbv, ALU.add),
                             waits=[stops[j], v_vo if g == 0 else None])
                st["bank_free"][j] = e
                if b == 2:
                    st["v_ready"][g] = e
        st["lnb_free"] = stops[7]

    tri_b = tri.rearrange("p (o c) -> p o c", c=128).broadcast_to([128, 2, 128])

    def flush_pv():
        if st["pend_pv"] is not None:
            f = st["pend_pv"]
            st["pend_pv"] = None
            f()
        if st["pend_epi"] is not None:
            f = st["pend_epi"]
            st["pend_epi"] = None
            f()

    def phase_B(g):
        nkt = 4 * g + 4
        first_sc = [True, True]
        first_pv = [True, True]
        for r in range(3):
            pin = r % 2
            pout = 1 - pin
            for j in range(8):
                sset = j % 2
                pva_s = pva0 if sset == 0 else pva1
                pvb_s = pvb0 if sset == 0 else pvb1
                pv_stop = [None]
                for kb in range(nkt):
                    i = st["pat"]
                    b = i % 2
                    scb = sc0 if b == 0 else sc1
                    jd = kb - 4 * g
                    C0 = 128 * jd if jd >= 0 else 0
                    qcol = pin * SEQ + j * 512
                    w_rhs = st["ctx_ready"][j] if r > 0 else st["q_ready"][j]
                    w_sc = [st["exp_buf"][b], w_rhs, st["k_ready"][kb // 4]]
                    if first_sc[b]:
                        first_sc[b] = False
                        w_sc += [st["bank_free"][2 * b], st["bank_free"][2 * b + 1]]
                    def MMT(out, lhsT, rhs, tp):
                        return lambda e, o=out, l=lhsT, r=rhs, t=tp: e.matmul(
                            o, l, r, start=True, stop=True, tile_position=t)
                    t_e = S.op("tensor", MMT(
                        scb[:, C0:512],
                        k_all[0:64, j * SEQ + kb * 128: j * SEQ + (kb + 1) * 128],
                        qc2[0:64, qcol + C0: qcol + 512], (0, 0)),
                        waits=w_sc)
                    t_o = S.op("tensor", MMT(
                        scb[:, 512 + C0:1024],
                        k_all[64:128, j * SEQ + kb * 128: j * SEQ + (kb + 1) * 128],
                        qc2[64:128, qcol + C0: qcol + 512], (64, 0)))
                    flush_pv()
                    ev = pair3(ebuf[:, b * 1024:(b + 1) * 1024])
                    s_e = S.op("scalar", ACT(ev[:, :, C0:512],
                                             pair3(scb)[:, :, C0:512], AF.Exp),
                               waits=[t_o])
                    st["exp_buf"][b] = s_e
                    wpv = s_e
                    if jd >= 0:
                        emv = ev[:, :, C0:C0 + 128]
                        wpv = S.op("vector", TT(emv, emv, tri_b, ALU.mult), waits=[s_e])
                    start = kb == 0
                    stop = kb == nkt - 1

                    def mk_pv(pva_s=pva_s, pvb_s=pvb_s, C0=C0, kb=kb, j=j, b=b,
                              start=start, stop=stop, wpv=wpv, sset=sset,
                              pv_stop=pv_stop, first_pv=first_pv):
                        def emit():
                            w = [wpv, st["v_ready"][kb // 4]]
                            if start:
                                w.append(st["pv_free"][sset])
                                if first_pv[sset]:
                                    first_pv[sset] = False
                                    w += [st["bank_free"][4 + 2 * sset],
                                          st["bank_free"][5 + 2 * sset]]
                            S.op("tensor", MM(
                                pva_s[0:65, C0:512],
                                v_all[:, kb * (16 * VW) + (2 * j) * VW: kb * (16 * VW) + (2 * j) * VW + VW],
                                ebuf[:, b * 1024 + C0: b * 1024 + 512], start, stop),
                                waits=w)
                            t2 = S.op("tensor", MM(
                                pvb_s[0:65, C0:512],
                                v_all[:, kb * (16 * VW) + (2 * j + 1) * VW: kb * (16 * VW) + (2 * j + 1) * VW + VW],
                                ebuf[:, b * 1024 + 512 + C0: b * 1024 + 1024], start, stop))
                            pv_stop[0] = t2
                        return emit
                    st["pend_pv"] = mk_pv()
                    st["pat"] += 1

                def mk_epi(pva_s=pva_s, pvb_s=pvb_s, j=j, sset=sset, pout=pout,
                           pv_stop=pv_stop, g=g):
                    def emit():
                        ocol = pout * SEQ + j * 512
                        if g <= 2:
                            s_la = S.op("scalar", ACT(rbuf[0:1, 0:512], pva_s[64:65, :],
                                                      AF.Ln),
                                        waits=[pv_stop[0], st["rbuf_free"]])
                            S.op("scalar", ACT(rbuf[0:1, 512:1024], pvb_s[64:65, :],
                                               AF.Ln))
                            v_rb = S.op("scalar", ACT(rbuf[0:1, 0:1024], rbuf[0:1, 0:1024],
                                                      AF.Exp, scale=-1.0))
                            t_ba = S.op("tensor", MM(pva_s[64:128, :], ones_r[0:1, 0:64],
                                                     rbuf[0:1, 0:512], True, True),
                                        waits=[v_rb])
                            t_bb = S.op("tensor", MM(pvb_s[64:128, :], ones_r[0:1, 0:64],
                                                     rbuf[0:1, 512:1024], True, True))
                            st["rbuf_free"] = t_bb
                            S.op("vector", CP(rb_sb[:, 0:512], pva_s[64:128, :]),
                                 waits=[t_ba, st["rb_free"]])
                            S.op("vector", CP(rb_sb[:, 512:1024], pvb_s[64:128, :]),
                                 waits=[t_bb])
                            v_ma = S.op("vector", TT(qc2[0:64, ocol:ocol + 512],
                                                     pva_s[0:64, :], rb_sb[:, 0:512],
                                                     ALU.mult))
                            v_mb = S.op("vector", TT(qc2[64:128, ocol:ocol + 512],
                                                     pvb_s[0:64, :], rb_sb[:, 512:1024],
                                                     ALU.mult))
                        else:
                            v_sa = S.op("vector", CP(sq[0:65, 0:512], pva_s[0:65, :]),
                                        waits=[pv_stop[0], st["sq_war"][0], st["sq_war"][1],
                                               st["sq_scalar_user"]])
                            v_sb = S.op("vector", CP(sq[0:65, 512:1024], pvb_s[0:65, :]))
                            st["pv_free"][sset] = v_sb
                            S.op("vector", RC(rbuf[0:1, 0:512], sq[64:65, 0:512]),
                                 waits=[st["rbuf_free"]])
                            v_rb = S.op("vector", RC(rbuf[0:1, 512:1024], sq[64:65, 512:1024]))
                            d1 = S.op("sync", DMA(SCR_R, rbuf[0:1, 0:1024]),
                                      waits=[v_rb, st["scr_free"]], sem="dbc", amt=16)
                            d2 = S.op("sync", DMA(
                                rb_sb[:, :],
                                SCR_R[0:1, 0:1024].partition_broadcast(64)[:, 0, :]),
                                waits=[d1, st["rb_free"]], sem="dbc", amt=16)
                            st["rbuf_free"] = d1
                            st["scr_free"] = d2
                            v_ma = S.op("vector", TT(qc2[0:64, ocol:ocol + 512],
                                                     sq[0:64, 0:512], rb_sb[:, 0:512],
                                                     ALU.mult),
                                        waits=[d2])
                            v_mb = S.op("vector", TT(qc2[64:128, ocol:ocol + 512],
                                                     sq[0:64, 512:1024], rb_sb[:, 512:1024],
                                                     ALU.mult))
                        st["rb_free"] = v_mb
                        st["ctx_ready"][j] = v_mb
                        if g <= 2:
                            st["pv_free"][sset] = v_mb
                    return emit
                prev = st["pend_epi"]
                assert prev is None
                st["pend_epi"] = mk_epi()
        flush_pv()
        owin_war = st["owin_war"]
        stops = [None] * 8
        for c in range(NCH):
            slot = c % 2
            wv = owin[:, slot * 1024:(slot + 1) * 1024]
            d = S.op("scalar", DMA(wv, OW_S[c * 128:(c + 1) * 128, :]),
                     waits=[ow_ready, owin_war[slot]], sem="dw%d" % slot, amt=16)
            for o in range(8):
                w = [d, st["ctx_ready"][c]]
                if c == 0:
                    w += [st["exp_buf"][0], st["exp_buf"][1],
                          st["pv_free"][0], st["pv_free"][1]]
                stops[o] = S.op("tensor", MM(banks[o], wv[:, o * 128:(o + 1) * 128],
                                             qc2[:, 1 * SEQ + c * 512: 1 * SEQ + (c + 1) * 512],
                                             c == 0, c == 7), waits=w)
            owin_war[slot] = stops[7]
        x2_ready = []
        for o in range(8):
            xo = xs[:, o * 512:(o + 1) * 512]
            v = S.op("vector", STT(xo, banks[o], outb[:, o:o + 1], xo, ALU.add, ALU.add),
                     waits=[stops[o]])
            st["bank_free"][o] = v
            x2_ready.append(v)
        st["x2_ready"] = x2_ready

    def phase_C(g):
        norm = emit_ln(None, g, None, chunk_ready=st["x2_ready"])
        wwin_war = st["wwin_war"]
        h_ready = [None] * 16
        for s in range(2):
            stops = [None] * 8
            for c in range(NCH):
                slot = c % 2
                wv = wwin[:, slot * 1024:(slot + 1) * 1024]
                d = S.op("scalar", DMA(wv, W1_S[c * 128:(c + 1) * 128, s * 1024:(s + 1) * 1024]),
                         waits=[w1_ready, wwin_war[slot]], sem="dw%d" % slot, amt=16)
                for j in range(8):
                    w = [d, norm[c]]
                    if c == 0:
                        w.append(st["bank_free"][j])
                    stops[j] = S.op("tensor", MM(banks[j], wv[:, j * 128:(j + 1) * 128],
                                                 lnb[:, c * 512:(c + 1) * 512],
                                                 c == 0, c == 7), waits=w)
                wwin_war[slot] = stops[7]
            for j in range(8):
                t = s * 8 + j
                e = S.op("vector", (lambda ee, o=qc2[:, t * 512:(t + 1) * 512],
                                    i=banks[j], bb=b1t[:, t:t + 1]:
                                    ee.tensor_scalar(o, i, bb, 0.0, op0=ALU.add,
                                                     op1=ALU.max)),
                         waits=[stops[j]])
                h_ready[t] = e
                st["bank_free"][j] = e
        st["lnb_free"] = stops[7]
        owin_war = st["owin_war"]
        stops = [None] * 8
        for hc in range(16):
            slot = hc % 2
            wv = owin[:, slot * 1024:(slot + 1) * 1024]
            d = S.op("scalar", DMA(wv, W2_S[hc * 128:(hc + 1) * 128, :]),
                     waits=[w2_ready, owin_war[slot]], sem="dw%d" % slot, amt=16)
            for o in range(8):
                w = [d, h_ready[hc]]
                if hc == 0:
                    w.append(st["bank_free"][o])
                stops[o] = S.op("tensor", MM(banks[o], wv[:, o * 128:(o + 1) * 128],
                                             qc2[:, hc * 512:(hc + 1) * 512],
                                             hc == 0, hc == 15), waits=w)
            owin_war[slot] = stops[7]
        st["qc2_free"] = stops[7]
        last_store = None
        for o in range(8):
            xo = xs[:, o * 512:(o + 1) * 512]
            v = S.op("vector", STT(xo, banks[o], b2t[:, o:o + 1], xo, ALU.add, ALU.add),
                     waits=[stops[o]])
            st["bank_free"][o] = v
            last_store = S.op("sync", DMA(OUT[o * 128:(o + 1) * 128, g * GT:(g + 1) * GT], xo),
                              waits=[v], sem="dout", amt=16)
        st["x_free"] = last_store

    for g in range(NG):
        phase_A(g)
        phase_B(g)
        phase_C(g)
    S.wait_only("sync", [("dout", S.cnt["dout"])])

    import bisect
    eng_set = {"tensor", "scalar", "vector", "gpsimd"}
    refs = {s: set() for s in eng_set}
    for e in ENGS:
        for waits, emit, sem, amt in S.items[e]:
            for (s, v) in waits:
                if s in refs:
                    refs[s].add(v)
    ms = {s: sorted(refs[s]) for s in eng_set}
    planned = {s: 0 for s in eng_set}
    for e in ENGS:
        new_items = []
        for waits, emit, sem, amt in S.items[e]:
            w2 = [(s, bisect.bisect_right(ms[s], v) if s in eng_set else v)
                  for (s, v) in waits]
            if sem in eng_set:
                planned[sem] += amt
                sem2 = sem if planned[sem] in refs[sem] else None
            else:
                sem2 = sem
            new_items.append((w2, emit, sem2, amt))
        S.items[e] = new_items

    with es:
        with nc.Block() as block:
            def runner(eng, name):
                lastw = {}
                for waits, emit, sem, amt in S.items[name]:
                    for (s, v) in waits:
                        if s == name:
                            continue
                        if lastw.get(s, 0) < v:
                            eng.wait_ge(sems[s], v)
                            lastw[s] = v
                    if emit is not None:
                        inst = emit(eng)
                        if sem is not None:
                            inst.then_inc(sems[sem], amt)

            @block.tensor
            def _(t):
                runner(t, "tensor")

            @block.scalar
            def _(s):
                runner(s, "scalar")

            @block.vector
            def _(v):
                runner(v, "vector")

            @block.gpsimd
            def _(gp):
                runner(gp, "gpsimd")

            @block.sync
            def _(sy):
                runner(sy, "sync")
    return nc


_CACHE = {}


def kernel(L0_attn_x, L0_attn_Wqkv_w, L0_attn_Wqkv_b, L0_attn_out_w, L0_attn_out_b,
           L0_ffn_w_gate, L0_ffn_w_noise, L0_ffn_W1, L0_ffn_b1, L0_ffn_W2, L0_ffn_b2):
    del L0_ffn_w_gate, L0_ffn_w_noise
    x = np.asarray(L0_attn_x, dtype=np.float32)
    wqkvT = np.ascontiguousarray(np.asarray(L0_attn_Wqkv_w, np.float32).T)
    qkv_b = np.asarray(L0_attn_Wqkv_b, np.float32)
    qb = np.ascontiguousarray(qkv_b.reshape(24, 128).T)
    vbr = np.ascontiguousarray(qkv_b[2048:3072].reshape(1, 1024))
    owT = np.ascontiguousarray(np.asarray(L0_attn_out_w, np.float32).T)
    ob = np.ascontiguousarray(np.asarray(L0_attn_out_b, np.float32).reshape(8, 128).T)
    w1 = np.ascontiguousarray(np.asarray(L0_ffn_W1, np.float32)[1])
    b1 = np.ascontiguousarray(np.asarray(L0_ffn_b1, np.float32)[1].reshape(16, 128).T)
    w2 = np.ascontiguousarray(np.asarray(L0_ffn_W2, np.float32)[1])
    b2 = np.ascontiguousarray(np.asarray(L0_ffn_b2, np.float32)[1].reshape(8, 128).T)
    tri = np.ascontiguousarray(np.triu(np.ones((128, 128), np.float32)))

    if "nc" not in _CACHE:
        _CACHE["nc"] = build_nc()
    nc = _CACHE["nc"]

    in_maps = []
    for b in range(8):
        in_maps.append({
            "xt": np.ascontiguousarray(x[b].T),
            "wq": wqkvT, "qb": qb, "vbr": vbr, "ow": owT, "ob": ob,
            "w1": w1, "b1": b1, "w2": w2, "b2": b2, "tri": tri,
        })
    res = run_bass_kernel_spmd(nc, in_maps, core_ids=list(range(8)))
    out = np.empty((8, SEQ, DM), np.float32)
    for b in range(8):
        out[b] = np.asarray(res.results[b]["outT"], np.float32).T
    return out


# version 19
# speedup vs baseline: 1.1984x; 1.1840x over previous
import numpy as np
from contextlib import ExitStack

import concourse.bass as bass
import concourse.mybir as mybir
from concourse.bass_utils import run_bass_kernel_spmd

F32 = mybir.dt.float32
BF16 = mybir.dt.bfloat16
AF = mybir.ActivationFunctionType
ALU = mybir.AluOpType

SEQ = 4096
DM = 1024
NG = 8
GT = 512
NCH = 8
NKT = 32
VW = 65

ENGS = ("tensor", "scalar", "vector", "gpsimd", "sync")


class Sched:

    def __init__(self):
        self.items = {e: [] for e in ENGS}
        self.cnt = {}
        self.pending = {e: [] for e in ENGS}
        self.gseq = []

    def op(self, eng, emit, waits=(), sem=None, amt=1):
        if sem is None:
            sem = eng
        self.cnt.setdefault(sem, 0)
        w = [x for x in self.pending[eng] if x is not None]
        self.pending[eng] = []
        w += [x for x in waits if x is not None]
        self.items[eng].append((w, emit, sem, amt))
        self.gseq.append((eng, len(self.items[eng]) - 1))
        self.cnt[sem] += amt
        return (sem, self.cnt[sem])

    def wait_only(self, eng, waits):
        self.items[eng].append(([x for x in waits if x is not None], None, None, 0))

    def barrier(self, engs=ENGS):
        marks = [(s, c) for s, c in self.cnt.items() if c > 0]
        for e in engs:
            self.pending[e] = marks[:]


def build_nc():
    nc = bass.Bass("TRN2", target_bir_lowering=False)

    XT = nc.dram_tensor("xt", [DM, SEQ], F32, kind="ExternalInput").ap()
    WQ = nc.dram_tensor("wq", [DM, 3 * DM], F32, kind="ExternalInput").ap()
    QB = nc.dram_tensor("qb", [128, 24], F32, kind="ExternalInput").ap()
    VBR = nc.dram_tensor("vbr", [1, DM], F32, kind="ExternalInput").ap()
    OW = nc.dram_tensor("ow", [DM, DM], F32, kind="ExternalInput").ap()
    OB = nc.dram_tensor("ob", [128, 8], F32, kind="ExternalInput").ap()
    W1 = nc.dram_tensor("w1", [DM, 2048], F32, kind="ExternalInput").ap()
    B1 = nc.dram_tensor("b1", [128, 16], F32, kind="ExternalInput").ap()
    W2 = nc.dram_tensor("w2", [2048, DM], F32, kind="ExternalInput").ap()
    B2 = nc.dram_tensor("b2", [128, 8], F32, kind="ExternalInput").ap()
    TRI = nc.dram_tensor("tri", [128, 128], F32, kind="ExternalInput").ap()
    OUT = nc.dram_tensor("outT", [DM, SEQ], F32, kind="ExternalOutput").ap()

    WQ_S = nc.dram_tensor("wq_s", [DM, 3 * DM], BF16, kind="Internal").ap()
    OW_S = nc.dram_tensor("ow_s", [DM, DM], BF16, kind="Internal").ap()
    W1_S = nc.dram_tensor("w1_s", [DM, 2048], BF16, kind="Internal").ap()
    W2_S = nc.dram_tensor("w2_s", [2048, DM], BF16, kind="Internal").ap()
    SCR_R = nc.dram_tensor("scr_r", [1, 1024], F32, kind="Internal").ap()

    es = ExitStack()
    sb = lambda name, shape, dt: es.enter_context(nc.sbuf_tensor(name, shape, dt))[:]
    ps = lambda name, shape: es.enter_context(nc.psum_tensor(name, shape, F32))[:]

    k_all = sb("k_all", [128, 8 * SEQ], BF16)
    v_all = sb("v_all", [128, NKT * 16 * VW], BF16)
    qc2 = sb("qc2", [128, 2 * SEQ], BF16)
    xs = sb("xs", [128, NCH * GT], F32)
    lnb = sb("lnb", [128, NCH * GT], BF16)
    wwin = sb("wwin", [128, 2 * 1024], BF16)
    owin = sb("owin", [128, 2 * 1024], BF16)
    ebuf = sb("ebuf", [128, 2 * 1024], BF16)
    rb_sb = sb("rb_sb", [64, 1024], F32)
    sq = sb("sq", [128, 2 * 512], F32)
    tmp_ln = sb("tmp_ln", [128, 512], F32)
    vbias = sb("vbias", [128, DM], F32)
    tri = sb("tri_sb", [128, 128], BF16)
    qkvb = sb("qkvb", [128, 24], F32)
    kbias = sb("kbias", [128, 8], F32)
    outb = sb("outb", [128, 8], F32)
    b1t = sb("b1t", [128, 16], F32)
    b2t = sb("b2t", [128, 8], F32)
    stats = sb("stats", [1, 3 * 512], F32)
    rbuf = sb("rbuf", [1, 1024], F32)
    ones_c = sb("ones_c", [128, 1], F32)
    ones_r = sb("ones_r", [1, 128], F32)
    eps_t = sb("eps_t", [1, 1], F32)

    sc0 = ps("sc0", [128, 1024])
    sc1 = ps("sc1", [128, 1024])
    pva0 = ps("pva0", [128, 512])
    pvb0 = ps("pvb0", [128, 512])
    pva1 = ps("pva1", [128, 512])
    pvb1 = ps("pvb1", [128, 512])
    banks = [sc0[:, 0:512], sc0[:, 512:1024], sc1[:, 0:512], sc1[:, 512:1024],
             pva0, pvb0, pva1, pvb1]

    sems = {}
    for s in ("tensor", "scalar", "vector", "gpsimd", "dx", "dw0", "dw1", "dg", "dgq",
              "dout", "dbc"):
        sems[s] = es.enter_context(nc.semaphore(s + "_sem"))

    S = Sched()
    st = {
        "bank_free": [None] * 8,
        "q_ready": [None] * 8,
        "ctx_ready": [None] * 8,
        "pv_free": [None, None],
        "exp_buf": [None, None],
        "pat": 0,
        "x_free": None,
        "x2_ready": None,
        "pend_pv": None,
        "pend_epi": None,
        "rbuf_free": None,
        "rb_free": None,
        "scr_free": None,
        "k_ready": [None] * NG,
        "v_ready": [None] * NG,
        "lnb_free": None,
        "qc2_free": None,
        "ln_free": None,
        "stats_free": None,
        "sq_war": [None, None],
        "sq_scalar_user": None,
        "wwin_war": [None, None],
        "owin_war": [None, None],
    }

    def MM(out, lhsT, rhs, start, stop):
        return lambda e, o=out, l=lhsT, r=rhs, s1=start, s2=stop: e.matmul(
            o, l, r, start=s1, stop=s2)

    def ACT(out, in_, func, bias=None, scale=None):
        kw = {}
        if bias is not None:
            kw["bias"] = bias
        if scale is not None:
            kw["scale"] = scale
        return lambda e, o=out, i=in_, f=func, k=dict(kw): e.activation(out=o, in_=i, func=f, **k)

    def TT(out, a, b, op):
        return lambda e, o=out, x=a, y=b, p=op: e.tensor_tensor(o, x, y, op=p)

    def STT(out, in0, scalar, in1, op0, op1):
        return lambda e, o=out, a=in0, s=scalar, b=in1, p0=op0, p1=op1: \
            e.scalar_tensor_tensor(o, a, s, b, op0=p0, op1=p1)

    def DMA(out, in_):
        return lambda e, o=out, i=in_: e.dma_start(out=o, in_=i)

    def CP(out, in_):
        return lambda e, o=out, i=in_: e.tensor_copy(o, i)

    def RC(out, in_):
        return lambda e, o=out, i=in_: e.reciprocal(o, i)

    def MS(ap, val):
        return lambda e, a=ap, v=val: e.memset(a, v)

    def MUL(out, in_, m):
        return lambda e, o=out, i=in_, v=m: e.mul(o, i, v)

    pair3 = lambda ap: ap.rearrange("p (o c) -> p o c", c=512)

    def dmy(out, waits, extra=()):
        S.op("tensor", MM(out, ones_r[0:1, 0:64], ones_r[0:1, 64:128], True, True),
             waits=list(waits) + list(extra))

    wq_ready = S.op("gpsimd", DMA(WQ_S, WQ), sem="dgq", amt=16)
    S.op("gpsimd", DMA(tri, TRI), sem="dg", amt=16)
    S.op("gpsimd", DMA(OW_S, OW), sem="dg", amt=16)
    S.op("gpsimd", DMA(W1_S, W1), sem="dg", amt=16)
    all_scratch = S.op("gpsimd", DMA(W2_S, W2), sem="dg", amt=16)
    tri_ld = ow_ready = w1_ready = w2_ready = all_scratch

    S.op("sync", DMA(qkvb, QB), sem="dx", amt=16)
    S.op("sync", DMA(rbuf[0:1, 0:1024], VBR), sem="dx", amt=16)
    S.op("sync", DMA(outb, OB), sem="dx", amt=16)
    S.op("sync", DMA(b1t, B1), sem="dx", amt=16)
    d_bias_all = S.op("sync", DMA(b2t, B2), sem="dx", amt=16)
    d_qb = d_vbr = d_bias_all

    v_oc = S.op("vector", MS(ones_c, 1.0 / DM))
    S.op("vector", MS(eps_t, 1e-5))
    v_or = S.op("vector", MS(ones_r, 1.0))
    vones_view = v_all.rearrange("p (t h c) -> p t h c", h=16, c=VW)[:, :, :, 64:65]
    v_vo = S.op("vector", MS(vones_view, 1.0))


    t_vb1 = S.op("tensor", MM(pva0, ones_r[0:1, :], rbuf[0:1, 0:512], True, True),
                 waits=[v_or, d_vbr])
    t_vb2 = S.op("tensor", MM(pvb0, ones_r[0:1, :], rbuf[0:1, 512:1024], True, True))
    S.op("vector", CP(vbias[:, 0:512], pva0), waits=[t_vb1])
    v_vb = S.op("vector", CP(vbias[:, 512:1024], pvb0), waits=[t_vb2])
    S.barrier()

    def emit_ln(src_dram, g, x_wait, chunk_ready=None):
        if src_dram is None:
            xld = list(chunk_ready)
        else:
            last = None
            for c in range(NCH):
                last = S.op("sync", DMA(xs[:, c * 512:(c + 1) * 512],
                                        src_dram[c * 128:(c + 1) * 128, g * GT:(g + 1) * GT]),
                            waits=[x_wait], sem="dx", amt=16)
                if c % 2 == 1:
                    dmy(pva1[0:64, 0:64], [last], [st["bank_free"][6]])
            xld = [last] * NCH
        t_sum = t_sq = None
        for c in range(NCH):
            xc = xs[:, c * 512:(c + 1) * 512]
            sqc = sq[:, (c % 2) * 512:((c % 2) + 1) * 512]
            v_s = S.op("vector", TT(sqc, xc, xc, ALU.mult),
                       waits=[xld[c], st["sq_war"][c % 2], st["sq_scalar_user"]])
            w = [xld[c]]
            if c == 0:
                w += [st["bank_free"][4], st["bank_free"][5]]
            t_sum = S.op("tensor", MM(pva0[0:1, :], ones_c, xc, c == 0, c == 7),
                         waits=w)
            t_sq = S.op("tensor", MM(pvb0[0:1, :], ones_c, sqc, c == 0, c == 7),
                        waits=[v_s])
            st["sq_war"][c % 2] = t_sq
        a_sq = S.op("scalar", (lambda e, o=stats[0:1, 1024:1536], i=pva0[0:1, :]:
                               e.activation(out=o, in_=i, func=AF.Square)),
                    waits=[t_sum, t_sq, st["stats_free"], st["ln_free"]])
        v_vr = S.op("vector", TT(stats[0:1, 512:1024], pvb0[0:1, :],
                                 stats[0:1, 1024:1536], ALU.subtract),
                    waits=[a_sq, st["stats_free"]])
        dmy(pva1[0:64, 0:64], [a_sq], [st["bank_free"][6]])
        a_ln = S.op("scalar", ACT(stats[0:1, 1024:1536], stats[0:1, 512:1024],
                                  AF.Ln, bias=eps_t[0:1, 0:1]), waits=[v_vr])
        a_rs = S.op("scalar", ACT(stats[0:1, 512:1024], stats[0:1, 1024:1536],
                                  AF.Exp, scale=-0.5))
        dmy(pva1[0:64, 0:64], [a_ln])
        v_mr = S.op("vector", TT(stats[0:1, 0:512], pva0[0:1, :],
                                 stats[0:1, 512:1024], ALU.mult),
                    waits=[a_rs, st["stats_free"]])
        a_m = a_q = a_rs
        t_b1 = S.op("tensor", MM(pva1, ones_r[0:1, :], stats[0:1, 512:1024], True, True),
                    waits=[a_rs, st["bank_free"][6]])
        t_b2 = S.op("tensor", MM(pvb1, ones_r[0:1, :], stats[0:1, 0:512], True, True),
                    waits=[v_mr, st["bank_free"][7]])
        norm = []
        for c in range(NCH):
            xc = xs[:, c * 512:(c + 1) * 512]
            S.op("vector", TT(tmp_ln, xc, pva1, ALU.mult),
                 waits=[t_b1, st["lnb_free"] if c == 0 else None])
            v2 = S.op("vector", TT(lnb[:, c * 512:(c + 1) * 512], tmp_ln, pvb1,
                                   ALU.subtract), waits=[t_b2])
            norm.append(v2)
        st["bank_free"][4] = v_mr
        st["bank_free"][5] = v_mr
        st["bank_free"][6] = norm[-1]
        st["bank_free"][7] = norm[-1]
        st["stats_free"] = t_b2
        st["ln_free"] = v_mr
        return norm

    def phase_A(g):
        norm = emit_ln(XT, g, st["x_free"])
        wwin_war = st["wwin_war"]
        for b in range(3):
            stops = [None] * 8
            for c in range(NCH):
                slot = c % 2
                wv = wwin[:, slot * 1024:(slot + 1) * 1024]
                d = S.op("scalar", DMA(wv, WQ_S[c * 128:(c + 1) * 128, b * 1024:(b + 1) * 1024]),
                         waits=[wq_ready, wwin_war[slot]], sem="dw%d" % slot, amt=16)
                for j in range(8):
                    if b < 2:
                        lhsT = wv[:, j * 128:(j + 1) * 128]
                        rhs = lnb[:, c * 512:(c + 1) * 512]
                    else:
                        tc, half = j // 2, j % 2
                        lhsT = lnb[:, c * 512 + tc * 128: c * 512 + (tc + 1) * 128]
                        rhs = wv[:, half * 512:(half + 1) * 512]
                    w = [d, norm[c]]
                    if c == 0:
                        w.append(st["bank_free"][j])
                    stops[j] = S.op("tensor", MM(banks[j], lhsT, rhs, c == 0, c == 7),
                                    waits=w)
                wwin_war[slot] = stops[7]
            for j in range(8):
                if b == 0:
                    e = S.op("vector", (lambda ee, o=qc2[:, j * 512:(j + 1) * 512],
                                        i=banks[j], bb=qkvb[:, j:j + 1]:
                                        ee.tensor_scalar(o, i, bb, None, op0=ALU.add)),
                             waits=[stops[j], st["qc2_free"] if j == 0 else None])
                    st["q_ready"][j] = e
                elif b == 1:
                    e = S.op("vector", (lambda ee, o=k_all[:, j * SEQ + g * 512: j * SEQ + (g + 1) * 512],
                                        i=banks[j], bb=qkvb[:, 8 + j:9 + j]:
                                        ee.tensor_scalar(o, i, bb, 0.125, op0=ALU.add,
                                                         op1=ALU.mult)),
                             waits=[stops[j]])
                    st["k_ready"][g] = e
                else:
                    tc, half = j // 2, j % 2
                    base = (g * 4 + tc) * (16 * VW) + half * 8 * VW
                    dst = v_all[:, base: base + 8 * VW].rearrange(
                        "p (h c) -> p h c", c=VW)[:, :, 0:64]
                    srcv = banks[j].rearrange("p (h c) -> p h c", c=64)
                    vbv = vbias[:, half * 512:(half + 1) * 512].rearrange(
                        "p (h c) -> p h c", c=64)
                    e = S.op("vector", TT(dst, srcv, vbv, ALU.add),
                             waits=[stops[j], v_vo if g == 0 else None])
                st["bank_free"][j] = e
                if b == 2:
                    st["v_ready"][g] = e
        st["lnb_free"] = stops[7]

    tri_b = tri.rearrange("p (o c) -> p o c", c=128).broadcast_to([128, 2, 128])

    def flush_pv():
        if st["pend_pv"] is not None:
            f = st["pend_pv"]
            st["pend_pv"] = None
            f()
        if st["pend_epi"] is not None:
            f = st["pend_epi"]
            st["pend_epi"] = None
            f()

    def phase_B(g):
        nkt = 4 * g + 4
        first_sc = [True, True]
        first_pv = [True, True]
        for r in range(3):
            pin = r % 2
            pout = 1 - pin
            for j in range(8):
                sset = j % 2
                pva_s = pva0 if sset == 0 else pva1
                pvb_s = pvb0 if sset == 0 else pvb1
                pv_stop = [None]
                for kb in range(nkt):
                    i = st["pat"]
                    b = i % 2
                    scb = sc0 if b == 0 else sc1
                    jd = kb - 4 * g
                    C0 = 128 * jd if jd >= 0 else 0
                    qcol = pin * SEQ + j * 512
                    w_rhs = st["ctx_ready"][j] if r > 0 else st["q_ready"][j]
                    w_sc = [st["exp_buf"][b], w_rhs, st["k_ready"][kb // 4]]
                    if first_sc[b]:
                        first_sc[b] = False
                        w_sc += [st["bank_free"][2 * b], st["bank_free"][2 * b + 1]]
                    def MMT(out, lhsT, rhs, tp):
                        return lambda e, o=out, l=lhsT, r=rhs, t=tp: e.matmul(
                            o, l, r, start=True, stop=True, tile_position=t)
                    t_e = S.op("tensor", MMT(
                        scb[:, C0:512],
                        k_all[0:64, j * SEQ + kb * 128: j * SEQ + (kb + 1) * 128],
                        qc2[0:64, qcol + C0: qcol + 512], (0, 0)),
                        waits=w_sc)
                    t_o = S.op("tensor", MMT(
                        scb[:, 512 + C0:1024],
                        k_all[64:128, j * SEQ + kb * 128: j * SEQ + (kb + 1) * 128],
                        qc2[64:128, qcol + C0: qcol + 512], (64, 0)))
                    flush_pv()
                    ev = pair3(ebuf[:, b * 1024:(b + 1) * 1024])
                    s_e = S.op("scalar", ACT(ev[:, :, C0:512],
                                             pair3(scb)[:, :, C0:512], AF.Exp),
                               waits=[t_o])
                    st["exp_buf"][b] = s_e
                    wpv = s_e
                    if jd >= 0:
                        emv = ev[:, :, C0:C0 + 128]
                        wpv = S.op("vector", TT(emv, emv, tri_b, ALU.mult), waits=[s_e])
                    start = kb == 0
                    stop = kb == nkt - 1

                    def mk_pv(pva_s=pva_s, pvb_s=pvb_s, C0=C0, kb=kb, j=j, b=b,
                              start=start, stop=stop, wpv=wpv, sset=sset,
                              pv_stop=pv_stop, first_pv=first_pv):
                        def emit():
                            w = [wpv, st["v_ready"][kb // 4]]
                            if start:
                                w.append(st["pv_free"][sset])
                                if first_pv[sset]:
                                    first_pv[sset] = False
                                    w += [st["bank_free"][4 + 2 * sset],
                                          st["bank_free"][5 + 2 * sset]]
                            S.op("tensor", MM(
                                pva_s[0:65, C0:512],
                                v_all[:, kb * (16 * VW) + (2 * j) * VW: kb * (16 * VW) + (2 * j) * VW + VW],
                                ebuf[:, b * 1024 + C0: b * 1024 + 512], start, stop),
                                waits=w)
                            t2 = S.op("tensor", MM(
                                pvb_s[0:65, C0:512],
                                v_all[:, kb * (16 * VW) + (2 * j + 1) * VW: kb * (16 * VW) + (2 * j + 1) * VW + VW],
                                ebuf[:, b * 1024 + 512 + C0: b * 1024 + 1024], start, stop))
                            pv_stop[0] = t2
                        return emit
                    st["pend_pv"] = mk_pv()
                    st["pat"] += 1

                def mk_epi(pva_s=pva_s, pvb_s=pvb_s, j=j, sset=sset, pout=pout,
                           pv_stop=pv_stop, g=g):
                    def emit():
                        ocol = pout * SEQ + j * 512
                        if g <= 2:
                            s_la = S.op("scalar", ACT(rbuf[0:1, 0:512], pva_s[64:65, :],
                                                      AF.Ln),
                                        waits=[pv_stop[0], st["rbuf_free"]])
                            S.op("scalar", ACT(rbuf[0:1, 512:1024], pvb_s[64:65, :],
                                               AF.Ln))
                            v_rb = S.op("scalar", ACT(rbuf[0:1, 0:1024], rbuf[0:1, 0:1024],
                                                      AF.Exp, scale=-1.0))
                            t_ba = S.op("tensor", MM(pva_s[64:128, :], ones_r[0:1, 0:64],
                                                     rbuf[0:1, 0:512], True, True),
                                        waits=[v_rb])
                            t_bb = S.op("tensor", MM(pvb_s[64:128, :], ones_r[0:1, 0:64],
                                                     rbuf[0:1, 512:1024], True, True))
                            st["rbuf_free"] = t_bb
                            S.op("vector", CP(rb_sb[:, 0:512], pva_s[64:128, :]),
                                 waits=[t_ba, st["rb_free"]])
                            S.op("vector", CP(rb_sb[:, 512:1024], pvb_s[64:128, :]),
                                 waits=[t_bb])
                            v_ma = S.op("vector", TT(qc2[0:64, ocol:ocol + 512],
                                                     pva_s[0:64, :], rb_sb[:, 0:512],
                                                     ALU.mult))
                            v_mb = S.op("vector", TT(qc2[64:128, ocol:ocol + 512],
                                                     pvb_s[0:64, :], rb_sb[:, 512:1024],
                                                     ALU.mult))
                        else:
                            v_sa = S.op("vector", CP(sq[0:65, 0:512], pva_s[0:65, :]),
                                        waits=[pv_stop[0], st["sq_war"][0], st["sq_war"][1],
                                               st["sq_scalar_user"]])
                            v_sb = S.op("vector", CP(sq[0:65, 512:1024], pvb_s[0:65, :]))
                            st["pv_free"][sset] = v_sb
                            S.op("vector", RC(rbuf[0:1, 0:512], sq[64:65, 0:512]),
                                 waits=[st["rbuf_free"]])
                            v_rb = S.op("vector", RC(rbuf[0:1, 512:1024], sq[64:65, 512:1024]))
                            d1 = S.op("sync", DMA(SCR_R, rbuf[0:1, 0:1024]),
                                      waits=[v_rb, st["scr_free"]], sem="dbc", amt=16)
                            d2 = S.op("sync", DMA(
                                rb_sb[:, :],
                                SCR_R[0:1, 0:1024].partition_broadcast(64)[:, 0, :]),
                                waits=[d1, st["rb_free"]], sem="dbc", amt=16)
                            st["rbuf_free"] = d1
                            st["scr_free"] = d2
                            v_ma = S.op("vector", TT(qc2[0:64, ocol:ocol + 512],
                                                     sq[0:64, 0:512], rb_sb[:, 0:512],
                                                     ALU.mult),
                                        waits=[d2])
                            v_mb = S.op("vector", TT(qc2[64:128, ocol:ocol + 512],
                                                     sq[0:64, 512:1024], rb_sb[:, 512:1024],
                                                     ALU.mult))
                        st["rb_free"] = v_mb
                        st["ctx_ready"][j] = v_mb
                        if g <= 2:
                            st["pv_free"][sset] = v_mb
                    return emit
                prev = st["pend_epi"]
                assert prev is None
                st["pend_epi"] = mk_epi()
        flush_pv()
        owin_war = st["owin_war"]
        stops = [None] * 8
        for c in range(NCH):
            slot = c % 2
            wv = owin[:, slot * 1024:(slot + 1) * 1024]
            d = S.op("scalar", DMA(wv, OW_S[c * 128:(c + 1) * 128, :]),
                     waits=[ow_ready, owin_war[slot]], sem="dw%d" % slot, amt=16)
            for o in range(8):
                w = [d, st["ctx_ready"][c]]
                if c == 0:
                    w += [st["exp_buf"][0], st["exp_buf"][1],
                          st["pv_free"][0], st["pv_free"][1]]
                stops[o] = S.op("tensor", MM(banks[o], wv[:, o * 128:(o + 1) * 128],
                                             qc2[:, 1 * SEQ + c * 512: 1 * SEQ + (c + 1) * 512],
                                             c == 0, c == 7), waits=w)
            owin_war[slot] = stops[7]
        x2_ready = []
        for o in range(8):
            xo = xs[:, o * 512:(o + 1) * 512]
            v = S.op("vector", STT(xo, banks[o], outb[:, o:o + 1], xo, ALU.add, ALU.add),
                     waits=[stops[o]])
            st["bank_free"][o] = v
            x2_ready.append(v)
        st["x2_ready"] = x2_ready

    def phase_C(g):
        norm = emit_ln(None, g, None, chunk_ready=st["x2_ready"])
        wwin_war = st["wwin_war"]
        h_ready = [None] * 16
        for s in range(2):
            stops = [None] * 8
            for c in range(NCH):
                slot = c % 2
                wv = wwin[:, slot * 1024:(slot + 1) * 1024]
                d = S.op("scalar", DMA(wv, W1_S[c * 128:(c + 1) * 128, s * 1024:(s + 1) * 1024]),
                         waits=[w1_ready, wwin_war[slot]], sem="dw%d" % slot, amt=16)
                for j in range(8):
                    w = [d, norm[c]]
                    if c == 0:
                        w.append(st["bank_free"][j])
                    stops[j] = S.op("tensor", MM(banks[j], wv[:, j * 128:(j + 1) * 128],
                                                 lnb[:, c * 512:(c + 1) * 512],
                                                 c == 0, c == 7), waits=w)
                wwin_war[slot] = stops[7]
            for j in range(8):
                t = s * 8 + j
                e = S.op("vector", (lambda ee, o=qc2[:, t * 512:(t + 1) * 512],
                                    i=banks[j], bb=b1t[:, t:t + 1]:
                                    ee.tensor_scalar(o, i, bb, 0.0, op0=ALU.add,
                                                     op1=ALU.max)),
                         waits=[stops[j]])
                h_ready[t] = e
                st["bank_free"][j] = e
        st["lnb_free"] = stops[7]
        owin_war = st["owin_war"]
        stops = [None] * 8
        for hc in range(16):
            slot = hc % 2
            wv = owin[:, slot * 1024:(slot + 1) * 1024]
            d = S.op("scalar", DMA(wv, W2_S[hc * 128:(hc + 1) * 128, :]),
                     waits=[w2_ready, owin_war[slot]], sem="dw%d" % slot, amt=16)
            for o in range(8):
                w = [d, h_ready[hc]]
                if hc == 0:
                    w.append(st["bank_free"][o])
                stops[o] = S.op("tensor", MM(banks[o], wv[:, o * 128:(o + 1) * 128],
                                             qc2[:, hc * 512:(hc + 1) * 512],
                                             hc == 0, hc == 15), waits=w)
            owin_war[slot] = stops[7]
        st["qc2_free"] = stops[7]
        last_store = None
        for o in range(8):
            xo = xs[:, o * 512:(o + 1) * 512]
            v = S.op("vector", STT(xo, banks[o], b2t[:, o:o + 1], xo, ALU.add, ALU.add),
                     waits=[stops[o]])
            st["bank_free"][o] = v
            last_store = S.op("sync", DMA(OUT[o * 128:(o + 1) * 128, g * GT:(g + 1) * GT], xo),
                              waits=[v], sem="dout", amt=16)
        st["x_free"] = last_store

    for g in range(NG):
        phase_A(g)
        phase_B(g)
        phase_C(g)
    S.wait_only("sync", [("dout", S.cnt["dout"])])

    import bisect
    eng_set = {"tensor", "scalar", "vector", "gpsimd"}
    refs = {s: set() for s in eng_set}
    for e in ENGS:
        for waits, emit, sem, amt in S.items[e]:
            for (s, v) in waits:
                if s in refs:
                    refs[s].add(v)
    ms = {s: sorted(refs[s]) for s in eng_set}
    planned = {s: 0 for s in eng_set}
    for e in ENGS:
        new_items = []
        for waits, emit, sem, amt in S.items[e]:
            w2 = [(s, bisect.bisect_right(ms[s], v) if s in eng_set else v)
                  for (s, v) in waits]
            if sem in eng_set:
                planned[sem] += amt
                sem2 = sem if planned[sem] in refs[sem] else None
            else:
                sem2 = sem
            new_items.append((w2, emit, sem2, amt))
        S.items[e] = new_items

    S_cnt_run = {}
    clk_ms = {}
    eng_clk = {e: {} for e in ENGS}
    for e, idx in S.gseq:
        waits, emit, sem, amt = S.items[e][idx]
        c = eng_clk[e]
        kept = []
        for (s, v) in waits:
            if c.get(s, 0) >= v:
                continue
            kept.append((s, v))
            mc = clk_ms.get((s, v))
            if mc:
                for (s2, v2) in mc.items():
                    if c.get(s2, 0) < v2:
                        c[s2] = v2
            if c.get(s, 0) < v:
                c[s] = v
        S.items[e][idx] = (kept, emit, sem, amt)
        if sem is not None:
            key = (sem, S_cnt_run.setdefault(sem, 0) + amt)
            S_cnt_run[sem] = key[1]
            clk_ms[key] = dict(c)
            c[sem] = max(c.get(sem, 0), key[1])

    with es:
        with nc.Block() as block:
            def runner(eng, name):
                lastw = {}
                for waits, emit, sem, amt in S.items[name]:
                    for (s, v) in waits:
                        if s == name:
                            continue
                        if lastw.get(s, 0) < v:
                            eng.wait_ge(sems[s], v)
                            lastw[s] = v
                    if emit is not None:
                        inst = emit(eng)
                        if sem is not None:
                            inst.then_inc(sems[sem], amt)

            @block.tensor
            def _(t):
                runner(t, "tensor")

            @block.scalar
            def _(s):
                runner(s, "scalar")

            @block.vector
            def _(v):
                runner(v, "vector")

            @block.gpsimd
            def _(gp):
                runner(gp, "gpsimd")

            @block.sync
            def _(sy):
                runner(sy, "sync")
    return nc


_CACHE = {}


def kernel(L0_attn_x, L0_attn_Wqkv_w, L0_attn_Wqkv_b, L0_attn_out_w, L0_attn_out_b,
           L0_ffn_w_gate, L0_ffn_w_noise, L0_ffn_W1, L0_ffn_b1, L0_ffn_W2, L0_ffn_b2):
    del L0_ffn_w_gate, L0_ffn_w_noise
    x = np.asarray(L0_attn_x, dtype=np.float32)
    wqkvT = np.ascontiguousarray(np.asarray(L0_attn_Wqkv_w, np.float32).T)
    qkv_b = np.asarray(L0_attn_Wqkv_b, np.float32)
    qb = np.ascontiguousarray(qkv_b.reshape(24, 128).T)
    vbr = np.ascontiguousarray(qkv_b[2048:3072].reshape(1, 1024))
    owT = np.ascontiguousarray(np.asarray(L0_attn_out_w, np.float32).T)
    ob = np.ascontiguousarray(np.asarray(L0_attn_out_b, np.float32).reshape(8, 128).T)
    w1 = np.ascontiguousarray(np.asarray(L0_ffn_W1, np.float32)[1])
    b1 = np.ascontiguousarray(np.asarray(L0_ffn_b1, np.float32)[1].reshape(16, 128).T)
    w2 = np.ascontiguousarray(np.asarray(L0_ffn_W2, np.float32)[1])
    b2 = np.ascontiguousarray(np.asarray(L0_ffn_b2, np.float32)[1].reshape(8, 128).T)
    tri = np.ascontiguousarray(np.triu(np.ones((128, 128), np.float32)))

    if "nc" not in _CACHE:
        _CACHE["nc"] = build_nc()
    nc = _CACHE["nc"]

    in_maps = []
    for b in range(8):
        in_maps.append({
            "xt": np.ascontiguousarray(x[b].T),
            "wq": wqkvT, "qb": qb, "vbr": vbr, "ow": owT, "ob": ob,
            "w1": w1, "b1": b1, "w2": w2, "b2": b2, "tri": tri,
        })
    res = run_bass_kernel_spmd(nc, in_maps, core_ids=list(range(8)))
    out = np.empty((8, SEQ, DM), np.float32)
    for b in range(8):
        out[b] = np.asarray(res.results[b]["outT"], np.float32).T
    return out
